# version 14
# baseline (speedup 1.0000x reference)
import contextlib
import numpy as np
import concourse.bass as bass
import concourse.mybir as mybir
from concourse.bass_utils import run_bass_kernel_spmd

F32 = mybir.dt.float32
F16 = mybir.dt.float16
AF = mybir.ActivationFunctionType
ALU = mybir.AluOpType
AX = mybir.AxisListType

NCORES = 8
TOK = 4096
H = 2048
N = 8192
TILE = 256
NT = TOK // TILE
NCH = TILE // 128
C = 32
HD = 64
DD = 4
NDC = HD // DD
HC = H // 128
NC = N // 128
EPS = 1e-5

SEM_NAMES = [
    "s_pre", "s_misc", "s_x", "s_w0", "s_w1", "s_y",
    "s_lnv", "s_rstd", "s_norm", "s_lnq", "s_trp", "s_evp",
    "s_mm1", "s_gel", "s_mmh", "s_eev",
    "s_qkv", "s_qev", "s_smul", "s_exp", "s_g", "s_ctx", "s_trc",
    "s_op", "s_oadd", "s_dvc", "s_dvr", "s_dvp", "s_ctr", "s_evl", "s_zl",
]


class Emit:

    def __init__(self, sems, active, eng):
        self.sems = sems
        self.cnt = {k: 0 for k in sems}
        self.active = active
        self.eng = eng

    def wait(self, which, sem, val):
        if val <= 0:
            return
        if which == self.active:
            self.eng.wait_ge(self.sems[sem], val)

    def op(self, which, fn, inc=None, inc_n=1):
        if which == self.active:
            ins = fn(self.eng)
            if inc is not None:
                ins.then_inc(self.sems[inc], inc_n)
        if inc is not None:
            self.cnt[inc] += inc_n

    def c(self, sem):
        return self.cnt[sem]


def schedule(em, t):
    x_in, y_out = t["x_in"], t["y_out"]
    w0t, w1t, w0s, w1s = t["w0t"], t["w1t"], t["w0s"], t["w1s"]
    xbuf, xln_attn, dbuf = t["xbuf"], t["xln_attn"], t["dbuf"]
    xln_f = {"ln1": t["xln_f1"], "ln3": t["xln_f3"]}
    wsl = [t["wsl0"], t["wsl1"]]
    qkvt = t["qkvt"]
    sS = [t["sS0"], t["sS1"]]
    sE = [t["sE0"], t["sE1"]]
    scr = [t["scr0"], t["scr1"]]
    scr2 = [t["scr2a"], t["scr2b"]]
    zbuf, zibuf, pbuf, ctxT = (t["zbuf"], t["zibuf"], t["pbuf"], t["ctxT"])
    zl = t["zl"]
    ctx2 = [t["ctx0"], t["ctx1"]]
    wq16, bq16, bo16, wo16 = t["wq16"], t["bq16"], t["bo16"], t["wo16"]
    b0sb, b1sb, b1h, idt, ones = (t["b0sb"], t["b1sb"], t["b1h"], t["idt"],
                                  t["ones"])
    stats, mv, rstd, epst = t["stats"], t["mv"], t["rstd"], t["epst"]
    ps_m = [t["ps_m0"], t["ps_m1"]]
    ps_tr = t["ps_tr"]
    ps_te = t["ps_te"]
    ps_q = [t["ps_q0"], t["ps_q1"]]

    st = {
        "scr_last": {0: 0, 1: 0},
        "xlnf_reader": {"ln1": 0, "ln3": 0},
        "bank_free": {0: None, 1: None},
        "ctx_free": {0: None, 1: None},
        "te_par": 0,
        "te_free": {0: None, 1: None},
        "te_cnt": {0: 0, 1: 0},
        "qo_free": {0: None, 1: None},
        "qo_idx": 0,
        "scr2_last": {0: 0, 1: 0},
        "ln_g": 0,
        "qa_idx": 0,
        "ln_done": {},
        "pass_done": {},
        "qa_oadd": {},
        "x_loaded": {},
        "y_done": {},
        "xlnattn_reader": 0,
        "d_reader": 0,
        "wsl_free": {0: None, 1: None},
        "gel_end": 0,
    }

    def x_sl(i):
        return xbuf[:, (i % 2) * NCH * H:(i % 2 + 1) * NCH * H]

    npre = 0
    for j in range(0, NC, 8):
        em.op("pool", lambda g, j=j: g.dma_start(
            out=w0s[j:j + 8], in_=w0t[j:j + 8]), inc="s_pre", inc_n=16)
        npre += 1
    for hb in range(0, HC, 2):
        em.op("pool", lambda g, hb=hb: g.dma_start(
            out=w1s[hb:hb + 2], in_=w1t[hb:hb + 2]), inc="s_pre", inc_n=16)
        npre += 1
    for dst, src in ((wq16[0:64, :], t["wq_in"][:]),
                     (wq16[64:128, :], t["wq_in"][:]),
                     (bq16[:], t["bq_in"][:]),
                     (bo16[:], t["bo_in"][:]),
                     (wo16[:], t["wo_in"][:]),
                     (b0sb[:], t["b0_in"][:]),
                     (b1sb[:], t["b1_in"][:]),
                     (idt[:], t["id_in"][:])):
        em.op("pool", lambda g, dst=dst, src=src: g.dma_start(out=dst, in_=src),
              inc="s_pre", inc_n=16)
        npre += 1
    PRE = 16 * npre

    em.wait("dve", "s_pre", PRE)
    em.op("dve", lambda v: v.tensor_scalar(b1h[:], b1sb[:], 0.5, None, ALU.mult),
          inc="s_misc")
    em.op("dve", lambda v: v.memset(ones[:], 1.0), inc="s_misc")
    em.op("dve", lambda v: v.memset(epst[:], EPS), inc="s_misc")
    MISC = 3
    em.wait("pe", "s_misc", MISC)
    em.wait("pe", "s_pre", PRE)
    em.wait("act", "s_pre", PRE)
    em.wait("act", "s_misc", MISC)
    em.wait("sp", "s_pre", PRE)

    def x_load(i):
        for tc in range(NCH):
            if i >= 1:
                em.wait("sp", "s_y", st["y_done"].get(i - 1, 0))
            em.op("sp", lambda s, i=i, tc=tc: s.dma_start(
                out=x_sl(i)[:, tc * H:(tc + 1) * H],
                in_=x_in.rearrange("(a p) h -> a p h", p=128)[NCH * i + tc]),
                inc="s_x", inc_n=16)
            st["x_loaded"][(i, tc)] = em.c("s_x")

    def y_store(i):
        for tc in range(NCH):
            em.wait("sp", "s_evp", st["pass_done"][("f2", i)])
            em.op("sp", lambda s, i=i, tc=tc: s.dma_start(
                out=y_out.rearrange("(a p) h -> a p h", p=128)[NCH * i + tc],
                in_=x_sl(i)[:, tc * H:(tc + 1) * H]), inc="s_y", inc_n=16)
        st["y_done"][i] = em.c("s_y")

    def ln_chunk(tile, which, tc):
        dest = xln_attn if which == "ln2" else xln_f[which]
        g = st["ln_g"]
        st["ln_g"] += 1
        xc = x_sl(tile)[:, tc * H:(tc + 1) * H]
        base = em.c("s_lnv")
        if which == "ln1":
            em.wait("dve", "s_x", st["x_loaded"][(tile, tc)])
        elif which == "ln2":
            em.wait("dve", "s_evp", st["pass_done"][("f1", tile)])
        else:
            em.wait("dve", "s_oadd", st["qa_oadd"][tile])
        for i in range(4):
            em.wait("dve", "s_lnv", base + i)
            em.op("dve", lambda v, i=i, xc=xc: v.bn_stats(
                stats[:, i * 6:(i + 1) * 6], xc[:, i * 512:(i + 1) * 512]),
                inc="s_lnv")
        em.wait("dve", "s_lnv", base + 4)
        em.wait("dve", "s_lnq", g)
        em.op("dve", lambda v: v.bn_aggr(mv[:], stats[:]), inc="s_lnv")
        em.wait("act", "s_lnv", base + 5)
        em.wait("act", "s_norm", 4 * g)
        em.op("act", lambda s: s.activation(rstd[:], mv[:, 1:2], AF.Ln,
                                            bias=epst[:], scale=1.0),
              inc="s_lnq")
        em.wait("act", "s_lnq", em.c("s_lnq"))
        em.op("act", lambda s: s.activation(rstd[:], rstd[:], AF.Exp,
                                            bias=0.0, scale=-0.5),
              inc="s_rstd")
        for grp in range(4):
            gidx = em.c("s_trp")
            par = gidx % 2
            tmv = scr[par][:]
            em.wait("dve", "s_rstd", g + 1)
            em.wait("dve", "s_trp", st["scr_last"][par])
            em.op("dve", lambda v, tmv=tmv, xc=xc, grp=grp:
                  v.tensor_scalar(tmv, xc[:, grp * 512:(grp + 1) * 512],
                                  mv[:, 0:1], rstd[:],
                                  ALU.subtract, ALU.mult), inc="s_norm")
            em.wait("pe", "s_norm", em.c("s_norm"))
            em.wait("pe", "s_evl", gidx - 1)
            if which == "ln2":
                em.wait("pe", "s_qkv", st["xlnattn_reader"])
            else:
                em.wait("pe", "s_mm1", st["xlnf_reader"][which])
            for b in range(4):
                em.op("pe", lambda p, b=b, par=par, tmv=tmv: p.transpose(
                    ps_tr[:, par * 1024 + b * 128:par * 1024 + (b + 1) * 128],
                    tmv[:, b * 128:(b + 1) * 128], idt[:]),
                    inc="s_trp" if b == 3 else None)
            st["scr_last"][par] = em.c("s_trp")
            em.wait("act", "s_trp", gidx + 1)
            dst = dest[:].rearrange("p (hc t) -> p hc t", hc=HC)[
                :, grp * 4:(grp + 1) * 4, tc * 128:(tc + 1) * 128]
            src = ps_tr[:, par * 1024:par * 1024 + 512]\
                .rearrange("p (b t) -> p b t", b=4)
            em.op("act", lambda s, dst=dst, src=src: s.copy(dst, src),
                  inc="s_evl")

    def ln_done(tile, which):
        st["ln_done"][(which, tile)] = em.c("s_evl")

    def ffn_mm1(tile, lnkey):
        base_mm1 = em.c("s_mm1")
        base_gel = em.c("s_gel")
        xln_wait = st["ln_done"][lnkey]
        xln_src = xln_f[lnkey[0]]
        d_guard = st["d_reader"]
        base_w0 = em.c("s_w0")
        wsl_prev = dict(st["wsl_free"])

        def w0_dma(k):
            if k >= 2:
                em.wait("sp", "s_mm1", base_mm1 + 2 * k - 2)
            elif wsl_prev[k % 2] is not None:
                em.wait("sp", *wsl_prev[k % 2])
            em.op("sp", lambda s, k=k: s.dma_start(
                out=wsl[k % 2][:].rearrange("p (a f) -> p a f", a=2),
                in_=w0s[2 * k:2 * k + 2].rearrange("a p f -> p a f")),
                inc="s_w0", inc_n=16)
            st["wsl_free"][k % 2] = ("s_mm1", base_mm1 + 2 * k + 2)
        w0_dma(0)
        w0_dma(1)
        for j in range(NC):
            if j % 2 == 0 and j // 2 + 2 < NC // 2:
                w0_dma(j // 2 + 2)
            ps = ps_m[j % 2]
            em.wait("pe", "s_w0", base_w0 + 16 * (j // 2 + 1))
            if j == 0:
                em.wait("pe", "s_evl", xln_wait)
            bf = st["bank_free"][j % 2]
            if bf is not None:
                em.wait("pe", bf[0], bf[1])
            for hc in range(HC):
                em.op("pe", lambda pe, j=j, hc=hc, ps=ps: pe.matmul(
                    ps[:, 0:TILE],
                    wsl[(j // 2) % 2][:, (j % 2) * HC * 128 + hc * 128:
                                      (j % 2) * HC * 128 + (hc + 1) * 128],
                    xln_src[:, hc * TILE:(hc + 1) * TILE],
                    start=(hc == 0), stop=(hc == HC - 1)),
                    inc="s_mm1" if hc == HC - 1 else None)
            em.wait("act", "s_mm1", base_mm1 + j + 1)
            if j == 0:
                em.wait("act", "s_mmh", d_guard)
            em.op("act", lambda s, j=j, ps=ps: s.activation(
                dbuf[:, j * TILE:(j + 1) * TILE], ps[:, 0:TILE], AF.Gelu,
                bias=b0sb[:, j:j + 1], scale=1.0), inc="s_gel")
            st["bank_free"][j % 2] = ("s_gel", base_gel + j + 1)
            yield
        st["xlnf_reader"][lnkey[0]] = base_mm1 + NC
        st["gel_end"] = base_gel + NC

    def ffn_mm2(tile, kind):
        base_mmh = em.c("s_mmh")
        base_eev = em.c("s_eev")
        gel_end = st["gel_end"]
        base_w1 = em.c("s_w1")
        wsl_prev = dict(st["wsl_free"])
        xs = x_sl(tile)

        def w1_dma(k):
            if k >= 2:
                em.wait("sp", "s_mmh", base_mmh + k - 1)
            elif wsl_prev[k % 2] is not None:
                em.wait("sp", *wsl_prev[k % 2])
            em.op("sp", lambda s, k=k: s.dma_start(
                out=wsl[k % 2][:],
                in_=w1s[k // 2][:, (k % 2) * 4096:(k % 2) * 4096 + 4096]),
                inc="s_w1", inc_n=16)
            st["wsl_free"][k % 2] = ("s_mmh", base_mmh + k + 1)
        w1_dma(0)
        w1_dma(1)
        for hc in range(HC):
            ps = ps_m[hc % 2]
            if hc == 0:
                em.wait("pe", "s_gel", gel_end)
            bf = st["bank_free"][hc % 2]
            if bf is not None:
                em.wait("pe", bf[0], bf[1])
            for half in range(2):
                k = 2 * hc + half
                if k + 2 < 2 * HC:
                    w1_dma(k + 2)
                em.wait("pe", "s_w1", base_w1 + 16 * (k + 1))
                for jj in range(32):
                    j = half * 32 + jj
                    em.op("pe", lambda pe, k=k, jj=jj, j=j, ps=ps: pe.matmul(
                        ps[:, 0:TILE], wsl[k % 2][:, jj * 128:(jj + 1) * 128],
                        dbuf[:, j * TILE:(j + 1) * TILE],
                        start=(j == 0), stop=(j == NC - 1)),
                        inc="s_mmh" if jj == 31 else None)
            spar = hc % 2
            em.wait("act", "s_mmh", base_mmh + 2 * (hc + 1))
            em.wait("act", "s_trc", st["scr2_last"][spar])
            em.op("act", lambda s, ps=ps, spar=spar, hc=hc: s.activation(
                scr2[spar][:, 0:TILE], ps[:, 0:TILE], AF.Identity,
                bias=b1h[:, hc:hc + 1], scale=0.5), inc="s_eev")
            tpar = st["te_par"]
            st["te_par"] ^= 1
            em.wait("pe", "s_eev", base_eev + hc + 1)
            if st["te_free"][tpar] is not None:
                em.wait("pe", "s_evp", st["te_free"][tpar])
            for tc in range(NCH):
                em.op("pe", lambda pe, tc=tc, tpar=tpar, spar=spar:
                      pe.transpose(
                          ps_te[:, tpar * 1024 + tc * 128:
                                tpar * 1024 + (tc + 1) * 128],
                          scr2[spar][:, tc * 128:(tc + 1) * 128], idt[:]),
                      inc="s_trc" if tc == NCH - 1 else None)
            st["scr2_last"][spar] = em.c("s_trc")
            st["bank_free"][hc % 2] = ("s_eev", base_eev + hc + 1)
            em.wait("dve", "s_trc", em.c("s_trc"))
            xv = xs.rearrange("p (tc h) -> p tc h", tc=NCH)[
                :, :, hc * 128:(hc + 1) * 128]
            srcv = ps_te[:, tpar * 1024:tpar * 1024 + NCH * 128]\
                .rearrange("p (b t) -> p b t", b=NCH)
            em.op("dve", lambda v, xv=xv, srcv=srcv: v.tensor_tensor(
                xv, xv, srcv, ALU.add), inc="s_evp")
            st["te_free"][tpar] = em.c("s_evp")
            yield
        st["d_reader"] = base_mmh + 2 * HC
        st["pass_done"][(kind, tile)] = em.c("s_evp")

    def qa_tile(i, bg_step, pend_in):
        xs = x_sl(i)
        pend = pend_in
        for tc in range(NCH):
            A = st["qa_idx"]
            st["qa_idx"] += 1
            ctx = ctx2[A % 2]
            ln2_wait = st["ln_done"][("ln2", i)]
            for grp in range(16):
                qg = 16 * A + grp
                qo = st["qo_idx"] % 2
                st["qo_idx"] += 1
                psq = ps_q[qo]
                if grp == 0:
                    em.wait("pe", "s_evl", ln2_wait)
                if st["qo_free"][qo] is not None:
                    em.wait("pe", *st["qo_free"][qo])
                for hh in range(2):
                    c = grp * 2 + hh
                    cb = (c % 2) * 64
                    em.op("pe", lambda pe, c=c, hh=hh, psq=psq: pe.matmul(
                        psq[:, hh * 256:hh * 256 + 192],
                        ones[0:1, 0:128], bq16[0:1, c * 192:(c + 1) * 192],
                        start=True, stop=False))
                    em.op("pe", lambda pe, c=c, hh=hh, cb=cb, psq=psq, tc=tc:
                          pe.matmul(
                              psq[:, hh * 256:hh * 256 + 192],
                              xln_attn[cb:cb + 64,
                                       (c // 2) * TILE + tc * 128:
                                       (c // 2) * TILE + (tc + 1) * 128],
                              wq16[cb:cb + 64, c * 192:(c + 1) * 192],
                              start=False, stop=True),
                          inc="s_qkv" if hh == 1 else None)
                em.wait("act", "s_qkv", qg + 1)
                em.wait("act", "s_ctx", NDC * A)
                dst = qkvt[:].rearrange("p (c i) -> p c i", c=C)[
                    :, grp * 2:grp * 2 + 2, :]
                src = psq[:].rearrange("p (h i) -> p h i", h=2)[:, :, 0:192]
                em.op("act", lambda s, dst=dst, src=src: s.copy(dst, src),
                      inc="s_qev")
                st["qo_free"][qo] = ("s_qev", em.c("s_qev"))
                bg_step(2)
            qr = qkvt[:].rearrange("p (c i) -> p c i", c=C)

            def emit_smul(dc, A=A):
                e_idx = NDC * A + dc
                par = dc % 2
                q_ap = qr[:, :, dc * DD:(dc + 1) * DD]\
                    .unsqueeze(2).broadcast_to([128, C, C, DD])
                k_ap = qr[:, :, 64 + dc * DD:64 + (dc + 1) * DD]\
                    .unsqueeze(1).broadcast_to([128, C, C, DD])
                Sv = sS[par][:].rearrange("p (c e d) -> p c e d", c=C, e=C)
                Ev = sE[par][:].rearrange("p (c e d) -> p c e d", c=C, e=C)
                if dc == 0:
                    em.wait("dve", "s_qev", 16 * (A + 1))
                em.wait("dve", "s_exp", e_idx - 1)
                em.wait("dve", "s_ctx", e_idx - 1)
                em.op("dve", lambda v, Sv=Sv, q_ap=q_ap, k_ap=k_ap:
                      v.tensor_tensor(Sv, q_ap, k_ap, ALU.mult), inc="s_smul")
                em.wait("act", "s_smul", e_idx + 1)
                em.wait("act", "s_g", e_idx - 1)
                em.op("act", lambda s, Ev=Ev, Sv=Sv: s.activation(
                    Ev, Sv, AF.Exp, bias=0.0, scale=0.125), inc="s_exp")

            def emit_zpgc(dc, A=A, ctx=ctx):
                e_idx = NDC * A + dc
                par = dc % 2
                Ev = sE[par][:].rearrange("p (c e d) -> p c e d", c=C, e=C)
                Gv = sS[par][:].rearrange("p (c e d) -> p c e d", c=C, e=C)
                v_ap = qr[:, :, 128 + dc * DD:128 + (dc + 1) * DD]
                em.wait("dve", "s_exp", e_idx + 1)
                em.wait("dve", "s_zl", e_idx)
                zv = zbuf[:].rearrange("p (c d) -> p c d", c=C)
                em.op("dve", lambda v, zv=zv, Ev=Ev: v.tensor_reduce(
                    zv, Ev.transpose([0, 1, 3, 2]), AX.X, ALU.add),
                    inc="s_dvc")
                if dc + 1 < NDC:
                    emit_smul(dc + 1)
                em.wait("act", "s_dvc", e_idx + 1)
                em.op("act", lambda s: s.activation(zl[:], zbuf[:], AF.Ln,
                                                    bias=0.0, scale=1.0),
                      inc="s_zl")
                em.wait("act", "s_zl", e_idx + 1)
                em.wait("act", "s_dvp", e_idx)
                em.op("act", lambda s: s.activation(zibuf[:], zl[:], AF.Exp,
                                                    bias=0.0, scale=-1.0),
                      inc="s_dvr")
                pv = pbuf[:].rearrange("p (c d) -> p c d", c=C)
                ziv = zibuf[:].rearrange("p (c d) -> p c d", c=C)
                em.wait("dve", "s_dvr", e_idx + 1)
                em.wait("dve", "s_g", e_idx)
                em.op("dve", lambda v, pv=pv, v_ap=v_ap, ziv=ziv:
                      v.tensor_tensor(pv, v_ap, ziv, ALU.mult), inc="s_dvp")
                em.wait("dve", "s_dvp", e_idx + 1)
                em.op("dve", lambda v, Gv=Gv, Ev=Ev, pv=pv: v.tensor_tensor(
                    Gv, Ev, pv.unsqueeze(2).broadcast_to([128, C, C, DD]),
                    ALU.mult), inc="s_g")
                if dc == 0 and st["ctx_free"][A % 2] is not None:
                    em.wait("dve", *st["ctx_free"][A % 2])
                cv = ctx[:].rearrange("p (m d) -> p m d", m=C)[
                    :, :, dc * DD:(dc + 1) * DD]
                em.wait("dve", "s_g", e_idx + 1)

                def _ctxred(v, cv=cv, Gv=Gv):
                    with t["nc"].allow_low_precision(
                            reason="ctx feeds fp16 matmul operand"):
                        return v.tensor_reduce(
                            cv, Gv.transpose([0, 2, 3, 1]), AX.X, ALU.add)
                em.op("dve", _ctxred, inc="s_ctx")

            emit_smul(0)
            for dc in range(NDC):
                emit_zpgc(dc)
                if dc == 2 and pend is not None:
                    pend()
                    pend = None
                bg_step(14)

            def pend_fn(A=A, tc=tc, ctx=ctx):
                for grp in range(4):
                    gidx = em.c("s_trp")
                    par = gidx % 2
                    em.wait("pe", "s_ctx", NDC * (A + 1))
                    em.wait("pe", "s_evl", gidx - 1)
                    for b in range(8):
                        m = grp * 8 + b
                        em.op("pe", lambda pe, m=m, b=b, par=par: pe.transpose(
                            ps_tr[0:64, par * 1024 + b * 128:
                                  par * 1024 + (b + 1) * 128],
                            ctx[:, m * 64:(m + 1) * 64], idt[:]),
                            inc="s_trp" if b == 7 else None)
                    em.op("pe", lambda pe: pe.nop(), inc="s_ctr")
                    em.wait("act", "s_trp", gidx + 1)
                    em.wait("act", "s_op", 4 * A)
                    em.op("act", lambda s, grp=grp, par=par: s.copy(
                        ctxT[0:64, grp * 1024:(grp + 1) * 1024],
                        ps_tr[0:64, par * 1024:par * 1024 + 1024]),
                        inc="s_evl")
                st["ctx_free"][A % 2] = ("s_ctr", em.c("s_ctr"))
                evl_now = em.c("s_evl")
                for grp in range(4):
                    og = 4 * A + grp
                    qo = st["qo_idx"] % 2
                    st["qo_idx"] += 1
                    pso = ps_q[qo]
                    em.wait("pe", "s_evl", evl_now)
                    if st["qo_free"][qo] is not None:
                        em.wait("pe", *st["qo_free"][qo])
                    em.op("pe", lambda pe, grp=grp, pso=pso: pe.matmul(
                        pso[:], ones[0:1, 0:128],
                        bo16[0:1, grp * 512:(grp + 1) * 512],
                        start=True, stop=False))
                    for b in range(8):
                        m = grp * 8 + b
                        em.op("pe", lambda pe, m=m, b=b, pso=pso: pe.matmul(
                            pso[:, b * 64:(b + 1) * 64],
                            ctxT[0:64, m * 128:(m + 1) * 128], wo16[:],
                            start=False, stop=(b == 7)),
                            inc="s_op" if b == 7 else None)
                    em.wait("dve", "s_op", og + 1)
                    xv = xs[:, tc * H + grp * 512: tc * H + (grp + 1) * 512]
                    em.op("dve", lambda v, xv=xv, pso=pso: v.tensor_tensor(
                        xv, xv, pso[:], ALU.add), inc="s_oadd")
                    st["qo_free"][qo] = ("s_oadd", em.c("s_oadd"))
            pend = pend_fn
        st["xlnattn_reader"] = em.c("s_qkv")
        return pend

    def qa_finish(i, pend):
        if pend is not None:
            pend()
        st["qa_oadd"][i] = em.c("s_oadd")

    x_load(0)
    for tc in range(NCH):
        ln_chunk(0, "ln1", tc)
    ln_done(0, "ln1")
    for _ in ffn_mm1(0, ("ln1", 0)):
        pass
    for _ in ffn_mm2(0, "f1"):
        pass

    for tc in range(NCH):
        ln_chunk(0, "ln2", tc)
    ln_done(0, "ln2")
    for i in range(NT):

        def bg_items(i=i):
            if i >= 1:
                yield from ffn_mm1(i - 1, ("ln3", i - 1))
                yield from ffn_mm2(i - 1, "f2")
                yield lambda: y_store(i - 1)
            if i + 1 < NT:
                yield lambda: x_load(i + 1)
                for tc in range(NCH):
                    yield lambda tc=tc: ln_chunk(i + 1, "ln1", tc)
                yield lambda: ln_done(i + 1, "ln1")
                yield from ffn_mm1(i + 1, ("ln1", i + 1))
                yield from ffn_mm2(i + 1, "f1")
        gen = bg_items()
        n_bg = ((NC + HC + 1) if i >= 1 else 0) + \
               ((1 + NCH + 1 + NC + HC) if i + 1 < NT else 0)
        ticks = NCH * (16 * 2 + NDC * 14)
        emitted = [0]
        tick = [0]

        def bg_step(w):
            tick[0] += w
            target = (n_bg * tick[0]) // ticks
            if target - emitted[0] < 8 and target < n_bg:
                return
            while emitted[0] < target:
                try:
                    item = next(gen)
                except StopIteration:
                    emitted[0] = n_bg
                    return
                if callable(item):
                    item()
                emitted[0] += 1

        pend = qa_tile(i, bg_step, None)
        qa_finish(i, pend)
        for tc in range(NCH):
            ln_chunk(i, "ln3", tc)
        ln_done(i, "ln3")
        for item in gen:
            if callable(item):
                item()
        if i + 1 < NT:
            for tc in range(NCH):
                ln_chunk(i + 1, "ln2", tc)
            ln_done(i + 1, "ln2")

    for _ in ffn_mm1(NT - 1, ("ln3", NT - 1)):
        pass
    for _ in ffn_mm2(NT - 1, "f2"):
        pass
    y_store(NT - 1)

    em.wait("sp", "s_y", 16 * NCH * NT)


def build_nc():
    nc = bass.Bass("TRN2", target_bir_lowering=False)
    t = {}
    t["x_in"] = nc.dram_tensor("x", [TOK, H], F32, kind="ExternalInput").ap()
    t["w0t"] = nc.dram_tensor("w0t", [NC, 128, HC * 128], F32,
                              kind="ExternalInput").ap()
    t["w1t"] = nc.dram_tensor("w1t", [HC, 128, NC * 128], F32,
                              kind="ExternalInput").ap()
    t["wq_in"] = nc.dram_tensor("wq", [64, C * 192], F32,
                                kind="ExternalInput").ap()
    t["bq_in"] = nc.dram_tensor("bq", [1, C * 192], F32,
                                kind="ExternalInput").ap()
    t["b0_in"] = nc.dram_tensor("b0", [128, NC], F32, kind="ExternalInput").ap()
    t["b1_in"] = nc.dram_tensor("b1", [128, HC], F32, kind="ExternalInput").ap()
    t["wo_in"] = nc.dram_tensor("wo", [64, 64], F32, kind="ExternalInput").ap()
    t["bo_in"] = nc.dram_tensor("bo", [1, H], F32, kind="ExternalInput").ap()
    t["id_in"] = nc.dram_tensor("ident", [128, 128], F16,
                                kind="ExternalInput").ap()
    t["y_out"] = nc.dram_tensor("y", [TOK, H], F32, kind="ExternalOutput").ap()
    t["w0s"] = nc.dram_tensor("w0s", [NC, 128, HC * 128], F16,
                              kind="Internal").ap()
    t["w1s"] = nc.dram_tensor("w1s", [HC, 128, NC * 128], F16,
                              kind="Internal").ap()

    sbufs = [
        ("xbuf", [128, 2 * NCH * H], F32),
        ("xln_f1", [128, HC * TILE], F16),
        ("xln_f3", [128, HC * TILE], F16),
        ("xln_attn", [128, HC * TILE], F16),
        ("dbuf", [128, NC * TILE], F16),
        ("wsl0", [128, 4096], F16), ("wsl1", [128, 4096], F16),
        ("qkvt", [128, C * 192], F16),
        ("sS0", [128, C * C * DD], F16), ("sS1", [128, C * C * DD], F16),
        ("sE0", [128, C * C * DD], F16), ("sE1", [128, C * C * DD], F16),
        ("scr0", [128, 512], F16), ("scr1", [128, 512], F16),
        ("scr2a", [128, 512], F16), ("scr2b", [128, 512], F16),
        ("zbuf", [128, C * DD], F32), ("zibuf", [128, C * DD], F32),
        ("zl", [128, C * DD], F32),
        ("pbuf", [128, C * DD], F16),
        ("ctx0", [128, C * HD], F16), ("ctx1", [128, C * HD], F16),
        ("ctxT", [64, C * 128], F16),
        ("wq16", [128, C * 192], F16), ("bq16", [1, C * 192], F16),
        ("bo16", [1, H], F16), ("wo16", [64, 64], F16),
        ("b0sb", [128, NC], F32), ("b1sb", [128, HC], F32),
        ("b1h", [128, HC], F32),
        ("idt", [128, 128], F16), ("ones", [1, 128], F16),
        ("stats", [128, 24], F32), ("mv", [128, 2], F32),
        ("rstd", [128, 1], F32), ("epst", [128, 1], F32),
    ]
    psums = [
        ("ps_m0", [128, 512], F32), ("ps_m1", [128, 512], F32),
        ("ps_tr", [128, 2048], F16),
        ("ps_te", [128, 2048], F16),
        ("ps_q0", [128, 512], F32), ("ps_q1", [128, 512], F32),
    ]

    with contextlib.ExitStack() as stack:
        for name, shape, dt in sbufs:
            t[name] = stack.enter_context(nc.sbuf_tensor(name, shape, dt))[:]
        for name, shape, dt in psums:
            t[name] = stack.enter_context(nc.psum_tensor(name, shape, dt))[:]
        t["nc"] = nc
        sems = {}
        for s in SEM_NAMES:
            sems[s] = stack.enter_context(nc.semaphore(s))
        with nc.Block() as block:
            block.sync(lambda eng: schedule(Emit(sems, "sp", eng), t))
            block.gpsimd(lambda eng: schedule(Emit(sems, "pool", eng), t))
            block.vector(lambda eng: schedule(Emit(sems, "dve", eng), t))
            block.scalar(lambda eng: schedule(Emit(sems, "act", eng), t))
            block.tensor(lambda eng: schedule(Emit(sems, "pe", eng), t))
    return nc


def _host_layouts(L0_attn_Wqkv, L0_attn_bqkv, L0_attn_Wout, L0_attn_bout,
                  L0_ffn_W0, L0_ffn_bias0, L0_ffn_W1, L0_ffn_bias1):
    W0 = np.asarray(L0_ffn_W0, dtype=np.float32)
    W1 = np.asarray(L0_ffn_W1, dtype=np.float32)
    w0t = np.ascontiguousarray(
        W0.reshape(NC, 128, HC, 128).transpose(0, 3, 2, 1)
        .reshape(NC, 128, HC * 128))
    w1t = np.ascontiguousarray(
        W1.reshape(HC, 128, NC, 128).transpose(0, 3, 2, 1)
        .reshape(HC, 128, NC * 128))
    wq = np.ascontiguousarray(
        np.asarray(L0_attn_Wqkv, dtype=np.float32).transpose(1, 0, 2)
        .reshape(64, C * 192))
    bq = np.ascontiguousarray(
        np.asarray(L0_attn_bqkv, dtype=np.float32).reshape(1, C * 192))
    b0 = np.ascontiguousarray(
        np.asarray(L0_ffn_bias0, dtype=np.float32).reshape(NC, 128).T)
    b1 = np.ascontiguousarray(
        np.asarray(L0_ffn_bias1, dtype=np.float32).reshape(HC, 128).T)
    wo = np.ascontiguousarray(np.asarray(L0_attn_Wout, dtype=np.float32))
    bo = np.ascontiguousarray(
        np.tile(np.asarray(L0_attn_bout, dtype=np.float32), C).reshape(1, H))
    ident = np.eye(128, dtype=np.float16)
    return dict(w0t=w0t, w1t=w1t, wq=wq, bq=bq, b0=b0, b1=b1, wo=wo, bo=bo,
                ident=ident)


def kernel(L0_attn_x, L0_attn_Wqkv, L0_attn_bqkv, L0_attn_Wout, L0_attn_bout,
           L0_ffn_W0, L0_ffn_bias0, L0_ffn_W1, L0_ffn_bias1):
    x = np.ascontiguousarray(np.asarray(L0_attn_x, dtype=np.float32))
    consts = _host_layouts(L0_attn_Wqkv, L0_attn_bqkv, L0_attn_Wout,
                           L0_attn_bout, L0_ffn_W0, L0_ffn_bias0, L0_ffn_W1,
                           L0_ffn_bias1)
    nc = build_nc()
    in_maps = [dict(x=np.ascontiguousarray(x[i]), **consts)
               for i in range(NCORES)]
    res = run_bass_kernel_spmd(nc, in_maps, core_ids=list(range(NCORES)))
    return np.stack([r["y"] for r in res.results], axis=0)


# version 15
# speedup vs baseline: 1.0443x; 1.0443x over previous
import contextlib
import numpy as np
import concourse.bass as bass
import concourse.mybir as mybir
from concourse.bass_utils import run_bass_kernel_spmd

F32 = mybir.dt.float32
F16 = mybir.dt.float16
AF = mybir.ActivationFunctionType
ALU = mybir.AluOpType
AX = mybir.AxisListType

NCORES = 8
TOK = 4096
H = 2048
N = 8192
TILE = 256
NT = TOK // TILE
NCH = TILE // 128
C = 32
HD = 64
DD = 4
NDC = HD // DD
HC = H // 128
NC = N // 128
EPS = 1e-5

SEM_NAMES = [
    "s_pre", "s_misc", "s_x", "s_w0", "s_w1", "s_y",
    "s_lnv", "s_rstd", "s_norm", "s_lnq", "s_trp", "s_evp",
    "s_mm1", "s_gel", "s_mmh", "s_eev",
    "s_qkv", "s_qev", "s_smul", "s_exp", "s_g", "s_ctx", "s_trc",
    "s_op", "s_oadd", "s_dvc", "s_dvr", "s_dvp", "s_ctr", "s_evl", "s_zl",
]


class Emit:

    def __init__(self, sems, active, eng):
        self.sems = sems
        self.cnt = {k: 0 for k in sems}
        self.active = active
        self.eng = eng

    def wait(self, which, sem, val):
        if val <= 0:
            return
        if which == self.active:
            self.eng.wait_ge(self.sems[sem], val)

    def op(self, which, fn, inc=None, inc_n=1):
        if which == self.active:
            ins = fn(self.eng)
            if inc is not None:
                ins.then_inc(self.sems[inc], inc_n)
        if inc is not None:
            self.cnt[inc] += inc_n

    def c(self, sem):
        return self.cnt[sem]


def schedule(em, t):
    x_in, y_out = t["x_in"], t["y_out"]
    w0t, w1t, w0s, w1s = t["w0t"], t["w1t"], t["w0s"], t["w1s"]
    xbuf, xln_attn, dbuf = t["xbuf"], t["xln_attn"], t["dbuf"]
    xln_f = {"ln1": t["xln_f1"], "ln3": t["xln_f3"]}
    wsl = [t["wsl0"], t["wsl1"]]
    qkvt = t["qkvt"]
    sS = [t["sS0"], t["sS1"]]
    sE = [t["sE0"], t["sE1"]]
    scr = [t["scr0"], t["scr1"]]
    scr2 = [t["scr2a"], t["scr2b"]]
    zbuf, zibuf, pbuf, ctxT = (t["zbuf"], t["zibuf"], t["pbuf"], t["ctxT"])
    zl = t["zl"]
    ctx2 = [t["ctx0"], t["ctx1"]]
    wq16, bq16, bo16, wo16 = t["wq16"], t["bq16"], t["bo16"], t["wo16"]
    b0sb, b1sb, b1h, idt, ones = (t["b0sb"], t["b1sb"], t["b1h"], t["idt"],
                                  t["ones"])
    stats, mv, rstd, epst = t["stats"], t["mv"], t["rstd"], t["epst"]
    ps_m = [t["ps_m0"], t["ps_m1"]]
    ps_tr = t["ps_tr"]
    ps_te = t["ps_te"]
    ps_q = [t["ps_q0"], t["ps_q1"]]

    st = {
        "scr_last": {0: 0, 1: 0},
        "xlnf_reader": {"ln1": 0, "ln3": 0},
        "bank_free": {0: None, 1: None},
        "ctx_free": {0: None, 1: None},
        "te_par": 0,
        "te_free": {0: None, 1: None},
        "te_cnt": {0: 0, 1: 0},
        "qo_free": {0: None, 1: None},
        "qo_idx": 0,
        "scr2_last": {0: 0, 1: 0},
        "ln_g": 0,
        "qa_idx": 0,
        "ln_done": {},
        "pass_done": {},
        "qa_oadd": {},
        "x_loaded": {},
        "y_done": {},
        "xlnattn_reader": 0,
        "d_reader": 0,
        "wsl_free": {0: None, 1: None},
        "gel_end": 0,
    }

    def x_sl(i):
        return xbuf[:, (i % 2) * NCH * H:(i % 2 + 1) * NCH * H]

    npre = 0
    for j in range(0, NC, 8):
        em.op("pool", lambda g, j=j: g.dma_start(
            out=w0s[j:j + 8], in_=w0t[j:j + 8]), inc="s_pre", inc_n=16)
        npre += 1
    for hb in range(0, HC, 2):
        em.op("pool", lambda g, hb=hb: g.dma_start(
            out=w1s[hb:hb + 2], in_=w1t[hb:hb + 2]), inc="s_pre", inc_n=16)
        npre += 1
    for dst, src in ((wq16[0:64, :], t["wq_in"][:]),
                     (wq16[64:128, :], t["wq_in"][:]),
                     (bq16[:], t["bq_in"][:]),
                     (bo16[:], t["bo_in"][:]),
                     (wo16[:], t["wo_in"][:]),
                     (b0sb[:], t["b0_in"][:]),
                     (b1sb[:], t["b1_in"][:]),
                     (idt[:], t["id_in"][:])):
        em.op("pool", lambda g, dst=dst, src=src: g.dma_start(out=dst, in_=src),
              inc="s_pre", inc_n=16)
        npre += 1
    PRE = 16 * npre

    em.wait("dve", "s_pre", PRE)
    em.op("dve", lambda v: v.tensor_scalar(b1h[:], b1sb[:], 0.5, None, ALU.mult),
          inc="s_misc")
    em.op("dve", lambda v: v.memset(ones[:], 1.0), inc="s_misc")
    em.op("dve", lambda v: v.memset(epst[:], EPS), inc="s_misc")
    MISC = 3
    em.wait("pe", "s_misc", MISC)
    em.wait("pe", "s_pre", PRE)
    em.wait("act", "s_pre", PRE)
    em.wait("act", "s_misc", MISC)
    em.wait("sp", "s_pre", PRE)

    def x_load(i):
        for tc in range(NCH):
            if i >= 1:
                em.wait("sp", "s_y", st["y_done"].get(i - 1, 0))
            em.op("sp", lambda s, i=i, tc=tc: s.dma_start(
                out=x_sl(i)[:, tc * H:(tc + 1) * H],
                in_=x_in.rearrange("(a p) h -> a p h", p=128)[NCH * i + tc]),
                inc="s_x", inc_n=16)
            st["x_loaded"][(i, tc)] = em.c("s_x")

    def y_store(i):
        for tc in range(NCH):
            em.wait("sp", "s_evp", st["pass_done"][("f2", i)])
            em.op("sp", lambda s, i=i, tc=tc: s.dma_start(
                out=y_out.rearrange("(a p) h -> a p h", p=128)[NCH * i + tc],
                in_=x_sl(i)[:, tc * H:(tc + 1) * H]), inc="s_y", inc_n=16)
        st["y_done"][i] = em.c("s_y")

    def ln_chunk(tile, which, tc):
        dest = xln_attn if which == "ln2" else xln_f[which]
        g = st["ln_g"]
        st["ln_g"] += 1
        xc = x_sl(tile)[:, tc * H:(tc + 1) * H]
        base = em.c("s_lnv")
        if which == "ln1":
            em.wait("dve", "s_x", st["x_loaded"][(tile, tc)])
        elif which == "ln2":
            em.wait("dve", "s_evp", st["pass_done"][("f1", tile)])
        else:
            em.wait("dve", "s_oadd", st["qa_oadd"][tile])
        for i in range(4):
            em.wait("dve", "s_lnv", base + i)
            em.op("dve", lambda v, i=i, xc=xc: v.bn_stats(
                stats[:, i * 6:(i + 1) * 6], xc[:, i * 512:(i + 1) * 512]),
                inc="s_lnv")
        em.wait("dve", "s_lnv", base + 4)
        em.wait("dve", "s_lnq", g)
        em.op("dve", lambda v: v.bn_aggr(mv[:], stats[:]), inc="s_lnv")
        em.wait("act", "s_lnv", base + 5)
        em.wait("act", "s_norm", 4 * g)
        em.op("act", lambda s: s.activation(rstd[:], mv[:, 1:2], AF.Ln,
                                            bias=epst[:], scale=1.0),
              inc="s_lnq")
        em.wait("act", "s_lnq", em.c("s_lnq"))
        em.op("act", lambda s: s.activation(rstd[:], rstd[:], AF.Exp,
                                            bias=0.0, scale=-0.5),
              inc="s_rstd")
        for grp in range(4):
            gidx = em.c("s_trp")
            par = gidx % 2
            tmv = scr[par][:]
            em.wait("dve", "s_rstd", g + 1)
            em.wait("dve", "s_trp", st["scr_last"][par])
            em.op("dve", lambda v, tmv=tmv, xc=xc, grp=grp:
                  v.tensor_scalar(tmv, xc[:, grp * 512:(grp + 1) * 512],
                                  mv[:, 0:1], rstd[:],
                                  ALU.subtract, ALU.mult), inc="s_norm")
            em.wait("pe", "s_norm", em.c("s_norm"))
            em.wait("pe", "s_evl", gidx - 1)
            if which == "ln2":
                em.wait("pe", "s_qkv", st["xlnattn_reader"])
            else:
                em.wait("pe", "s_mm1", st["xlnf_reader"][which])
            for b in range(4):
                em.op("pe", lambda p, b=b, par=par, tmv=tmv: p.transpose(
                    ps_tr[:, par * 1024 + b * 128:par * 1024 + (b + 1) * 128],
                    tmv[:, b * 128:(b + 1) * 128], idt[:]),
                    inc="s_trp" if b == 3 else None)
            st["scr_last"][par] = em.c("s_trp")
            em.wait("act", "s_trp", gidx + 1)
            dst = dest[:].rearrange("p (hc t) -> p hc t", hc=HC)[
                :, grp * 4:(grp + 1) * 4, tc * 128:(tc + 1) * 128]
            src = ps_tr[:, par * 1024:par * 1024 + 512]\
                .rearrange("p (b t) -> p b t", b=4)
            em.op("act", lambda s, dst=dst, src=src: s.copy(dst, src),
                  inc="s_evl")

    def ln_done(tile, which):
        st["ln_done"][(which, tile)] = em.c("s_evl")

    def ffn_mm1(tile, lnkey):
        base_mm1 = em.c("s_mm1")
        base_gel = em.c("s_gel")
        xln_wait = st["ln_done"][lnkey]
        xln_src = xln_f[lnkey[0]]
        d_guard = st["d_reader"]
        base_w0 = em.c("s_w0")
        wsl_prev = dict(st["wsl_free"])

        def w0_dma(k):
            if k >= 2:
                em.wait("sp", "s_mm1", base_mm1 + 2 * k - 2)
            elif wsl_prev[k % 2] is not None:
                em.wait("sp", *wsl_prev[k % 2])
            em.op("sp", lambda s, k=k: s.dma_start(
                out=wsl[k % 2][:].rearrange("p (a f) -> p a f", a=2),
                in_=w0s[2 * k:2 * k + 2].rearrange("a p f -> p a f")),
                inc="s_w0", inc_n=16)
            st["wsl_free"][k % 2] = ("s_mm1", base_mm1 + 2 * k + 2)
        w0_dma(0)
        w0_dma(1)
        for j in range(NC):
            if j % 2 == 0 and j // 2 + 2 < NC // 2:
                w0_dma(j // 2 + 2)
            ps = ps_m[j % 2]
            em.wait("pe", "s_w0", base_w0 + 16 * (j // 2 + 1))
            if j == 0:
                em.wait("pe", "s_evl", xln_wait)
            bf = st["bank_free"][j % 2]
            if bf is not None:
                em.wait("pe", bf[0], bf[1])
            for hc in range(HC):
                em.op("pe", lambda pe, j=j, hc=hc, ps=ps: pe.matmul(
                    ps[:, 0:TILE],
                    wsl[(j // 2) % 2][:, (j % 2) * HC * 128 + hc * 128:
                                      (j % 2) * HC * 128 + (hc + 1) * 128],
                    xln_src[:, hc * TILE:(hc + 1) * TILE],
                    start=(hc == 0), stop=(hc == HC - 1)),
                    inc="s_mm1" if hc == HC - 1 else None)
            em.wait("act", "s_mm1", base_mm1 + j + 1)
            if j == 0:
                em.wait("act", "s_mmh", d_guard)
            em.op("act", lambda s, j=j, ps=ps: s.activation(
                dbuf[:, j * TILE:(j + 1) * TILE], ps[:, 0:TILE], AF.Gelu,
                bias=b0sb[:, j:j + 1], scale=1.0), inc="s_gel")
            st["bank_free"][j % 2] = ("s_gel", base_gel + j + 1)
            yield
        st["xlnf_reader"][lnkey[0]] = base_mm1 + NC
        st["gel_end"] = base_gel + NC

    def ffn_mm2(tile, kind):
        base_mmh = em.c("s_mmh")
        base_eev = em.c("s_eev")
        gel_end = st["gel_end"]
        base_w1 = em.c("s_w1")
        wsl_prev = dict(st["wsl_free"])
        xs = x_sl(tile)

        def w1_dma(k):
            if k >= 2:
                em.wait("sp", "s_mmh", base_mmh + k - 1)
            elif wsl_prev[k % 2] is not None:
                em.wait("sp", *wsl_prev[k % 2])
            em.op("sp", lambda s, k=k: s.dma_start(
                out=wsl[k % 2][:],
                in_=w1s[k // 2][:, (k % 2) * 4096:(k % 2) * 4096 + 4096]),
                inc="s_w1", inc_n=16)
            st["wsl_free"][k % 2] = ("s_mmh", base_mmh + k + 1)
        w1_dma(0)
        w1_dma(1)
        for hc in range(HC):
            ps = ps_m[hc % 2]
            if hc == 0:
                em.wait("pe", "s_gel", gel_end)
            bf = st["bank_free"][hc % 2]
            if bf is not None:
                em.wait("pe", bf[0], bf[1])
            for half in range(2):
                k = 2 * hc + half
                if k + 2 < 2 * HC:
                    w1_dma(k + 2)
                em.wait("pe", "s_w1", base_w1 + 16 * (k + 1))
                for jj in range(32):
                    j = half * 32 + jj
                    em.op("pe", lambda pe, k=k, jj=jj, j=j, ps=ps: pe.matmul(
                        ps[:, 0:TILE], wsl[k % 2][:, jj * 128:(jj + 1) * 128],
                        dbuf[:, j * TILE:(j + 1) * TILE],
                        start=(j == 0), stop=(j == NC - 1)),
                        inc="s_mmh" if jj == 31 else None)
            spar = hc % 2
            em.wait("act", "s_mmh", base_mmh + 2 * (hc + 1))
            em.wait("act", "s_trc", st["scr2_last"][spar])
            em.op("act", lambda s, ps=ps, spar=spar, hc=hc: s.activation(
                scr2[spar][:, 0:TILE], ps[:, 0:TILE], AF.Identity,
                bias=b1h[:, hc:hc + 1], scale=0.5), inc="s_eev")
            tpar = st["te_par"]
            st["te_par"] ^= 1
            em.wait("pe", "s_eev", base_eev + hc + 1)
            if st["te_free"][tpar] is not None:
                em.wait("pe", "s_evp", st["te_free"][tpar])
            for tc in range(NCH):
                em.op("pe", lambda pe, tc=tc, tpar=tpar, spar=spar:
                      pe.transpose(
                          ps_te[:, tpar * 1024 + tc * 128:
                                tpar * 1024 + (tc + 1) * 128],
                          scr2[spar][:, tc * 128:(tc + 1) * 128], idt[:]),
                      inc="s_trc" if tc == NCH - 1 else None)
            st["scr2_last"][spar] = em.c("s_trc")
            st["bank_free"][hc % 2] = ("s_eev", base_eev + hc + 1)
            em.wait("dve", "s_trc", em.c("s_trc"))
            xv = xs.rearrange("p (tc h) -> p tc h", tc=NCH)[
                :, :, hc * 128:(hc + 1) * 128]
            srcv = ps_te[:, tpar * 1024:tpar * 1024 + NCH * 128]\
                .rearrange("p (b t) -> p b t", b=NCH)
            em.op("dve", lambda v, xv=xv, srcv=srcv: v.tensor_tensor(
                xv, xv, srcv, ALU.add), inc="s_evp")
            st["te_free"][tpar] = em.c("s_evp")
            yield
        st["d_reader"] = base_mmh + 2 * HC
        st["pass_done"][(kind, tile)] = em.c("s_evp")

    def qa_tile(i, bg_step, pend_in):
        xs = x_sl(i)
        pend = pend_in
        for tc in range(NCH):
            A = st["qa_idx"]
            st["qa_idx"] += 1
            ctx = ctx2[A % 2]
            ln2_wait = st["ln_done"][("ln2", i)]
            for grp in range(16):
                qg = 16 * A + grp
                qo = st["qo_idx"] % 2
                st["qo_idx"] += 1
                psq = ps_q[qo]
                if grp == 0:
                    em.wait("pe", "s_evl", ln2_wait)
                if st["qo_free"][qo] is not None:
                    em.wait("pe", *st["qo_free"][qo])
                for hh in range(2):
                    c = grp * 2 + hh
                    cb = (c % 2) * 64
                    em.op("pe", lambda pe, c=c, hh=hh, psq=psq: pe.matmul(
                        psq[:, hh * 256:hh * 256 + 192],
                        ones[0:1, 0:128], bq16[0:1, c * 192:(c + 1) * 192],
                        start=True, stop=False))
                    em.op("pe", lambda pe, c=c, hh=hh, cb=cb, psq=psq, tc=tc:
                          pe.matmul(
                              psq[:, hh * 256:hh * 256 + 192],
                              xln_attn[cb:cb + 64,
                                       (c // 2) * TILE + tc * 128:
                                       (c // 2) * TILE + (tc + 1) * 128],
                              wq16[cb:cb + 64, c * 192:(c + 1) * 192],
                              start=False, stop=True),
                          inc="s_qkv" if hh == 1 else None)
                em.wait("act", "s_qkv", qg + 1)
                em.wait("act", "s_ctx", NDC * A)
                dst = qkvt[:].rearrange("p (c i) -> p c i", c=C)[
                    :, grp * 2:grp * 2 + 2, :]
                src = psq[:].rearrange("p (h i) -> p h i", h=2)[:, :, 0:192]
                em.op("act", lambda s, dst=dst, src=src: s.copy(dst, src),
                      inc="s_qev")
                st["qo_free"][qo] = ("s_qev", em.c("s_qev"))
                bg_step(2)
            qr = qkvt[:].rearrange("p (c i) -> p c i", c=C)

            def emit_smul(dc, A=A):
                e_idx = NDC * A + dc
                par = dc % 2
                q_ap = qr[:, :, dc * DD:(dc + 1) * DD]\
                    .unsqueeze(2).broadcast_to([128, C, C, DD])
                k_ap = qr[:, :, 64 + dc * DD:64 + (dc + 1) * DD]\
                    .unsqueeze(1).broadcast_to([128, C, C, DD])
                Sv = sS[par][:].rearrange("p (c e d) -> p c e d", c=C, e=C)
                Ev = sE[par][:].rearrange("p (c e d) -> p c e d", c=C, e=C)
                if dc == 0:
                    em.wait("dve", "s_qev", 16 * (A + 1))
                em.wait("dve", "s_exp", e_idx - 1)
                em.wait("dve", "s_ctx", e_idx - 1)
                em.op("dve", lambda v, Sv=Sv, q_ap=q_ap, k_ap=k_ap:
                      v.tensor_tensor(Sv, q_ap, k_ap, ALU.mult), inc="s_smul")
                em.wait("act", "s_smul", e_idx + 1)
                em.wait("act", "s_g", e_idx - 1)
                em.op("act", lambda s, Ev=Ev, Sv=Sv: s.activation(
                    Ev, Sv, AF.Exp, bias=0.0, scale=0.125), inc="s_exp")

            def emit_zpgc(dc, A=A, ctx=ctx):
                e_idx = NDC * A + dc
                par = dc % 2
                Ev = sE[par][:].rearrange("p (c e d) -> p c e d", c=C, e=C)
                Gv = sS[par][:].rearrange("p (c e d) -> p c e d", c=C, e=C)
                v_ap = qr[:, :, 128 + dc * DD:128 + (dc + 1) * DD]
                em.wait("dve", "s_exp", e_idx + 1)
                em.wait("dve", "s_dvr", e_idx)
                zv = zbuf[:].rearrange("p (c d) -> p c d", c=C)
                em.op("dve", lambda v, zv=zv, Ev=Ev: v.tensor_reduce(
                    zv, Ev.transpose([0, 1, 3, 2]), AX.X, ALU.add),
                    inc="s_dvc")
                if dc + 1 < NDC:
                    emit_smul(dc + 1)
                em.wait("dve", "s_dvc", e_idx + 1)
                em.wait("dve", "s_dvp", e_idx)
                em.op("dve", lambda v: v.reciprocal(zibuf[:], zbuf[:]),
                      inc="s_dvr")
                pv = pbuf[:].rearrange("p (c d) -> p c d", c=C)
                ziv = zibuf[:].rearrange("p (c d) -> p c d", c=C)
                em.wait("dve", "s_dvr", e_idx + 1)
                em.wait("dve", "s_g", e_idx)
                em.op("dve", lambda v, pv=pv, v_ap=v_ap, ziv=ziv:
                      v.tensor_tensor(pv, v_ap, ziv, ALU.mult), inc="s_dvp")
                em.wait("dve", "s_dvp", e_idx + 1)
                em.op("dve", lambda v, Gv=Gv, Ev=Ev, pv=pv: v.tensor_tensor(
                    Gv, Ev, pv.unsqueeze(2).broadcast_to([128, C, C, DD]),
                    ALU.mult), inc="s_g")
                if dc == 0 and st["ctx_free"][A % 2] is not None:
                    em.wait("dve", *st["ctx_free"][A % 2])
                cv = ctx[:].rearrange("p (m d) -> p m d", m=C)[
                    :, :, dc * DD:(dc + 1) * DD]
                em.wait("dve", "s_g", e_idx + 1)

                def _ctxred(v, cv=cv, Gv=Gv):
                    with t["nc"].allow_low_precision(
                            reason="ctx feeds fp16 matmul operand"):
                        return v.tensor_reduce(
                            cv, Gv.transpose([0, 2, 3, 1]), AX.X, ALU.add)
                em.op("dve", _ctxred, inc="s_ctx")

            emit_smul(0)
            for dc in range(NDC):
                emit_zpgc(dc)
                if dc == 2 and pend is not None:
                    pend()
                    pend = None
                bg_step(14)

            def pend_fn(A=A, tc=tc, ctx=ctx):
                for grp in range(4):
                    gidx = em.c("s_trp")
                    par = gidx % 2
                    em.wait("pe", "s_ctx", NDC * (A + 1))
                    em.wait("pe", "s_evl", gidx - 1)
                    for b in range(8):
                        m = grp * 8 + b
                        em.op("pe", lambda pe, m=m, b=b, par=par: pe.transpose(
                            ps_tr[0:64, par * 1024 + b * 128:
                                  par * 1024 + (b + 1) * 128],
                            ctx[:, m * 64:(m + 1) * 64], idt[:]),
                            inc="s_trp" if b == 7 else None)
                    em.op("pe", lambda pe: pe.nop(), inc="s_ctr")
                    em.wait("act", "s_trp", gidx + 1)
                    em.wait("act", "s_op", 4 * A)
                    em.op("act", lambda s, grp=grp, par=par: s.copy(
                        ctxT[0:64, grp * 1024:(grp + 1) * 1024],
                        ps_tr[0:64, par * 1024:par * 1024 + 1024]),
                        inc="s_evl")
                st["ctx_free"][A % 2] = ("s_ctr", em.c("s_ctr"))
                evl_now = em.c("s_evl")
                for grp in range(4):
                    og = 4 * A + grp
                    qo = st["qo_idx"] % 2
                    st["qo_idx"] += 1
                    pso = ps_q[qo]
                    em.wait("pe", "s_evl", evl_now)
                    if st["qo_free"][qo] is not None:
                        em.wait("pe", *st["qo_free"][qo])
                    em.op("pe", lambda pe, grp=grp, pso=pso: pe.matmul(
                        pso[:], ones[0:1, 0:128],
                        bo16[0:1, grp * 512:(grp + 1) * 512],
                        start=True, stop=False))
                    for b in range(8):
                        m = grp * 8 + b
                        em.op("pe", lambda pe, m=m, b=b, pso=pso: pe.matmul(
                            pso[:, b * 64:(b + 1) * 64],
                            ctxT[0:64, m * 128:(m + 1) * 128], wo16[:],
                            start=False, stop=(b == 7)),
                            inc="s_op" if b == 7 else None)
                    em.wait("dve", "s_op", og + 1)
                    xv = xs[:, tc * H + grp * 512: tc * H + (grp + 1) * 512]
                    em.op("dve", lambda v, xv=xv, pso=pso: v.tensor_tensor(
                        xv, xv, pso[:], ALU.add), inc="s_oadd")
                    st["qo_free"][qo] = ("s_oadd", em.c("s_oadd"))
                    bg_step(3)
            pend = pend_fn
        st["xlnattn_reader"] = em.c("s_qkv")
        return pend

    def qa_finish(i, pend):
        if pend is not None:
            pend()
        st["qa_oadd"][i] = em.c("s_oadd")

    x_load(0)
    for tc in range(NCH):
        ln_chunk(0, "ln1", tc)
    ln_done(0, "ln1")
    for _ in ffn_mm1(0, ("ln1", 0)):
        pass
    for _ in ffn_mm2(0, "f1"):
        pass

    for tc in range(NCH):
        ln_chunk(0, "ln2", tc)
    ln_done(0, "ln2")
    for i in range(NT):

        def bg_items(i=i):
            if i >= 1:
                yield from ffn_mm1(i - 1, ("ln3", i - 1))
                yield from ffn_mm2(i - 1, "f2")
                yield lambda: y_store(i - 1)
            if i + 1 < NT:
                yield lambda: x_load(i + 1)
                for tc in range(NCH):
                    yield lambda tc=tc: ln_chunk(i + 1, "ln1", tc)
                yield lambda: ln_done(i + 1, "ln1")
                yield from ffn_mm1(i + 1, ("ln1", i + 1))
                yield from ffn_mm2(i + 1, "f1")
        gen = bg_items()
        n_bg = ((NC + HC + 1) if i >= 1 else 0) + \
               ((1 + NCH + 1 + NC + HC) if i + 1 < NT else 0)
        ticks = NCH * (16 * 2 + NDC * 14)
        emitted = [0]
        tick = [0]

        def bg_step(w):
            tick[0] += w
            target = min(n_bg, (n_bg * tick[0] * 13) // (ticks * 10))
            if target - emitted[0] < 8 and target < n_bg:
                return
            while emitted[0] < target:
                try:
                    item = next(gen)
                except StopIteration:
                    emitted[0] = n_bg
                    return
                if callable(item):
                    item()
                emitted[0] += 1

        pend = qa_tile(i, bg_step, None)
        qa_finish(i, pend)
        for tc in range(NCH):
            ln_chunk(i, "ln3", tc)
        ln_done(i, "ln3")
        for item in gen:
            if callable(item):
                item()
        if i + 1 < NT:
            for tc in range(NCH):
                ln_chunk(i + 1, "ln2", tc)
            ln_done(i + 1, "ln2")

    for _ in ffn_mm1(NT - 1, ("ln3", NT - 1)):
        pass
    for _ in ffn_mm2(NT - 1, "f2"):
        pass
    y_store(NT - 1)

    em.wait("sp", "s_y", 16 * NCH * NT)


def build_nc():
    nc = bass.Bass("TRN2", target_bir_lowering=False)
    t = {}
    t["x_in"] = nc.dram_tensor("x", [TOK, H], F32, kind="ExternalInput").ap()
    t["w0t"] = nc.dram_tensor("w0t", [NC, 128, HC * 128], F32,
                              kind="ExternalInput").ap()
    t["w1t"] = nc.dram_tensor("w1t", [HC, 128, NC * 128], F32,
                              kind="ExternalInput").ap()
    t["wq_in"] = nc.dram_tensor("wq", [64, C * 192], F32,
                                kind="ExternalInput").ap()
    t["bq_in"] = nc.dram_tensor("bq", [1, C * 192], F32,
                                kind="ExternalInput").ap()
    t["b0_in"] = nc.dram_tensor("b0", [128, NC], F32, kind="ExternalInput").ap()
    t["b1_in"] = nc.dram_tensor("b1", [128, HC], F32, kind="ExternalInput").ap()
    t["wo_in"] = nc.dram_tensor("wo", [64, 64], F32, kind="ExternalInput").ap()
    t["bo_in"] = nc.dram_tensor("bo", [1, H], F32, kind="ExternalInput").ap()
    t["id_in"] = nc.dram_tensor("ident", [128, 128], F16,
                                kind="ExternalInput").ap()
    t["y_out"] = nc.dram_tensor("y", [TOK, H], F32, kind="ExternalOutput").ap()
    t["w0s"] = nc.dram_tensor("w0s", [NC, 128, HC * 128], F16,
                              kind="Internal").ap()
    t["w1s"] = nc.dram_tensor("w1s", [HC, 128, NC * 128], F16,
                              kind="Internal").ap()

    sbufs = [
        ("xbuf", [128, 2 * NCH * H], F32),
        ("xln_f1", [128, HC * TILE], F16),
        ("xln_f3", [128, HC * TILE], F16),
        ("xln_attn", [128, HC * TILE], F16),
        ("dbuf", [128, NC * TILE], F16),
        ("wsl0", [128, 4096], F16), ("wsl1", [128, 4096], F16),
        ("qkvt", [128, C * 192], F16),
        ("sS0", [128, C * C * DD], F16), ("sS1", [128, C * C * DD], F16),
        ("sE0", [128, C * C * DD], F16), ("sE1", [128, C * C * DD], F16),
        ("scr0", [128, 512], F16), ("scr1", [128, 512], F16),
        ("scr2a", [128, 512], F16), ("scr2b", [128, 512], F16),
        ("zbuf", [128, C * DD], F32), ("zibuf", [128, C * DD], F32),
        ("zl", [128, C * DD], F32),
        ("pbuf", [128, C * DD], F16),
        ("ctx0", [128, C * HD], F16), ("ctx1", [128, C * HD], F16),
        ("ctxT", [64, C * 128], F16),
        ("wq16", [128, C * 192], F16), ("bq16", [1, C * 192], F16),
        ("bo16", [1, H], F16), ("wo16", [64, 64], F16),
        ("b0sb", [128, NC], F32), ("b1sb", [128, HC], F32),
        ("b1h", [128, HC], F32),
        ("idt", [128, 128], F16), ("ones", [1, 128], F16),
        ("stats", [128, 24], F32), ("mv", [128, 2], F32),
        ("rstd", [128, 1], F32), ("epst", [128, 1], F32),
    ]
    psums = [
        ("ps_m0", [128, 512], F32), ("ps_m1", [128, 512], F32),
        ("ps_tr", [128, 2048], F16),
        ("ps_te", [128, 2048], F16),
        ("ps_q0", [128, 512], F32), ("ps_q1", [128, 512], F32),
    ]

    with contextlib.ExitStack() as stack:
        for name, shape, dt in sbufs:
            t[name] = stack.enter_context(nc.sbuf_tensor(name, shape, dt))[:]
        for name, shape, dt in psums:
            t[name] = stack.enter_context(nc.psum_tensor(name, shape, dt))[:]
        t["nc"] = nc
        sems = {}
        for s in SEM_NAMES:
            sems[s] = stack.enter_context(nc.semaphore(s))
        with nc.Block() as block:
            block.sync(lambda eng: schedule(Emit(sems, "sp", eng), t))
            block.gpsimd(lambda eng: schedule(Emit(sems, "pool", eng), t))
            block.vector(lambda eng: schedule(Emit(sems, "dve", eng), t))
            block.scalar(lambda eng: schedule(Emit(sems, "act", eng), t))
            block.tensor(lambda eng: schedule(Emit(sems, "pe", eng), t))
    return nc


def _host_layouts(L0_attn_Wqkv, L0_attn_bqkv, L0_attn_Wout, L0_attn_bout,
                  L0_ffn_W0, L0_ffn_bias0, L0_ffn_W1, L0_ffn_bias1):
    W0 = np.asarray(L0_ffn_W0, dtype=np.float32)
    W1 = np.asarray(L0_ffn_W1, dtype=np.float32)
    w0t = np.ascontiguousarray(
        W0.reshape(NC, 128, HC, 128).transpose(0, 3, 2, 1)
        .reshape(NC, 128, HC * 128))
    w1t = np.ascontiguousarray(
        W1.reshape(HC, 128, NC, 128).transpose(0, 3, 2, 1)
        .reshape(HC, 128, NC * 128))
    wq = np.ascontiguousarray(
        np.asarray(L0_attn_Wqkv, dtype=np.float32).transpose(1, 0, 2)
        .reshape(64, C * 192))
    bq = np.ascontiguousarray(
        np.asarray(L0_attn_bqkv, dtype=np.float32).reshape(1, C * 192))
    b0 = np.ascontiguousarray(
        np.asarray(L0_ffn_bias0, dtype=np.float32).reshape(NC, 128).T)
    b1 = np.ascontiguousarray(
        np.asarray(L0_ffn_bias1, dtype=np.float32).reshape(HC, 128).T)
    wo = np.ascontiguousarray(np.asarray(L0_attn_Wout, dtype=np.float32))
    bo = np.ascontiguousarray(
        np.tile(np.asarray(L0_attn_bout, dtype=np.float32), C).reshape(1, H))
    ident = np.eye(128, dtype=np.float16)
    return dict(w0t=w0t, w1t=w1t, wq=wq, bq=bq, b0=b0, b1=b1, wo=wo, bo=bo,
                ident=ident)


def kernel(L0_attn_x, L0_attn_Wqkv, L0_attn_bqkv, L0_attn_Wout, L0_attn_bout,
           L0_ffn_W0, L0_ffn_bias0, L0_ffn_W1, L0_ffn_bias1):
    x = np.ascontiguousarray(np.asarray(L0_attn_x, dtype=np.float32))
    consts = _host_layouts(L0_attn_Wqkv, L0_attn_bqkv, L0_attn_Wout,
                           L0_attn_bout, L0_ffn_W0, L0_ffn_bias0, L0_ffn_W1,
                           L0_ffn_bias1)
    nc = build_nc()
    in_maps = [dict(x=np.ascontiguousarray(x[i]), **consts)
               for i in range(NCORES)]
    res = run_bass_kernel_spmd(nc, in_maps, core_ids=list(range(NCORES)))
    return np.stack([r["y"] for r in res.results], axis=0)


# version 16
# speedup vs baseline: 1.1191x; 1.0716x over previous
import contextlib
import numpy as np
import concourse.bass as bass
import concourse.mybir as mybir
from concourse.bass_utils import run_bass_kernel_spmd

F32 = mybir.dt.float32
F16 = mybir.dt.float16
AF = mybir.ActivationFunctionType
ALU = mybir.AluOpType
AX = mybir.AxisListType

NCORES = 8
TOK = 4096
H = 2048
N = 8192
TILE = 256
NT = TOK // TILE
NCH = TILE // 128
C = 32
HD = 64
DD = 4
NDC = HD // DD
HC = H // 128
NC = N // 128
EPS = 1e-5

SEM_NAMES = [
    "s_pre", "s_misc", "s_x", "s_w0", "s_w1", "s_y",
    "s_lnv", "s_rstd", "s_norm", "s_lnq", "s_trp", "s_evp",
    "s_mm1", "s_gel", "s_mmh", "s_eev",
    "s_qkv", "s_qev", "s_smul", "s_exp", "s_g", "s_ctx", "s_trc",
    "s_op", "s_oadd", "s_dvc", "s_dvr", "s_dvp", "s_ctr", "s_evl", "s_zl",
]


class Emit:

    def __init__(self, sems, active, eng):
        self.sems = sems
        self.cnt = {k: 0 for k in sems}
        self.active = active
        self.eng = eng

    def wait(self, which, sem, val):
        if val <= 0:
            return
        if which == self.active:
            self.eng.wait_ge(self.sems[sem], val)

    def op(self, which, fn, inc=None, inc_n=1):
        if which == self.active:
            ins = fn(self.eng)
            if inc is not None:
                ins.then_inc(self.sems[inc], inc_n)
        if inc is not None:
            self.cnt[inc] += inc_n

    def c(self, sem):
        return self.cnt[sem]


def schedule(em, t):
    x_in, y_out = t["x_in"], t["y_out"]
    w0t, w1t, w0s, w1s = t["w0t"], t["w1t"], t["w0s"], t["w1s"]
    xbuf, xln_attn, dbuf = t["xbuf"], t["xln_attn"], t["dbuf"]
    xln_f = {"ln1": t["xln_f1"], "ln3": t["xln_f3"]}
    wsl = [t["wsl0"], t["wsl1"], t["wsl2"]]
    qkvt = t["qkvt"]
    sS = [t["sS0"], t["sS1"]]
    sE = [t["sE0"], t["sE1"]]
    scr = [t["scr0"], t["scr1"]]
    scr2 = [t["scr2a"], t["scr2b"]]
    zbuf, zibuf, pbuf, ctxT = (t["zbuf"], t["zibuf"], t["pbuf"], t["ctxT"])
    zl = t["zl"]
    ctx2 = [t["ctx0"], t["ctx1"]]
    wq16, bq16, bo16, wo16 = t["wq16"], t["bq16"], t["bo16"], t["wo16"]
    b0sb, b1sb, b1h, idt, ones = (t["b0sb"], t["b1sb"], t["b1h"], t["idt"],
                                  t["ones"])
    stats, mv, rstd, epst = t["stats"], t["mv"], t["rstd"], t["epst"]
    ps_m = [t["ps_m0"], t["ps_m1"]]
    ps_tr = t["ps_tr"]
    ps_te = t["ps_te"]
    ps_q = [t["ps_q0"], t["ps_q1"]]

    st = {
        "scr_last": {0: 0, 1: 0},
        "xlnf_reader": {"ln1": 0, "ln3": 0},
        "bank_free": {0: None, 1: None},
        "ctx_free": {0: None, 1: None},
        "te_par": 0,
        "te_free": {0: None, 1: None},
        "te_cnt": {0: 0, 1: 0},
        "qo_free": {0: None, 1: None},
        "qo_idx": 0,
        "scr2_last": {0: 0, 1: 0},
        "ln_g": 0,
        "qa_idx": 0,
        "ln_done": {},
        "pass_done": {},
        "qa_oadd": {},
        "x_loaded": {},
        "y_done": {},
        "xlnattn_reader": 0,
        "d_reader": 0,
        "wsl_free": {0: None, 1: None, 2: None},
        "gel_end": 0,
    }

    def x_sl(i):
        return xbuf[:, (i % 2) * NCH * H:(i % 2 + 1) * NCH * H]

    npre = 0
    for j in range(0, NC, 8):
        em.op("pool", lambda g, j=j: g.dma_start(
            out=w0s[j:j + 8], in_=w0t[j:j + 8]), inc="s_pre", inc_n=16)
        npre += 1
    for hb in range(0, HC, 2):
        em.op("pool", lambda g, hb=hb: g.dma_start(
            out=w1s[hb:hb + 2], in_=w1t[hb:hb + 2]), inc="s_pre", inc_n=16)
        npre += 1
    for dst, src in ((wq16[0:64, :], t["wq_in"][:]),
                     (wq16[64:128, :], t["wq_in"][:]),
                     (bq16[:], t["bq_in"][:]),
                     (bo16[:], t["bo_in"][:]),
                     (wo16[:], t["wo_in"][:]),
                     (b0sb[:], t["b0_in"][:]),
                     (b1sb[:], t["b1_in"][:]),
                     (idt[:], t["id_in"][:])):
        em.op("pool", lambda g, dst=dst, src=src: g.dma_start(out=dst, in_=src),
              inc="s_pre", inc_n=16)
        npre += 1
    PRE = 16 * npre

    em.wait("dve", "s_pre", PRE)
    em.op("dve", lambda v: v.tensor_scalar(b1h[:], b1sb[:], 0.5, None, ALU.mult),
          inc="s_misc")
    em.op("dve", lambda v: v.memset(ones[:], 1.0), inc="s_misc")
    em.op("dve", lambda v: v.memset(epst[:], EPS), inc="s_misc")
    MISC = 3
    em.wait("pe", "s_misc", MISC)
    em.wait("pe", "s_pre", PRE)
    em.wait("act", "s_pre", PRE)
    em.wait("act", "s_misc", MISC)
    em.wait("sp", "s_pre", PRE)

    def x_load(i):
        for tc in range(NCH):
            if i >= 1:
                em.wait("sp", "s_y", st["y_done"].get(i - 1, 0))
            em.op("sp", lambda s, i=i, tc=tc: s.dma_start(
                out=x_sl(i)[:, tc * H:(tc + 1) * H],
                in_=x_in.rearrange("(a p) h -> a p h", p=128)[NCH * i + tc]),
                inc="s_x", inc_n=16)
            st["x_loaded"][(i, tc)] = em.c("s_x")

    def y_store(i):
        for tc in range(NCH):
            em.wait("sp", "s_evp", st["pass_done"][("f2", i)])
            em.op("sp", lambda s, i=i, tc=tc: s.dma_start(
                out=y_out.rearrange("(a p) h -> a p h", p=128)[NCH * i + tc],
                in_=x_sl(i)[:, tc * H:(tc + 1) * H]), inc="s_y", inc_n=16)
        st["y_done"][i] = em.c("s_y")

    def ln_chunk(tile, which, tc):
        dest = xln_attn if which == "ln2" else xln_f[which]
        g = st["ln_g"]
        st["ln_g"] += 1
        xc = x_sl(tile)[:, tc * H:(tc + 1) * H]
        base = em.c("s_lnv")
        if which == "ln1":
            em.wait("dve", "s_x", st["x_loaded"][(tile, tc)])
        elif which == "ln2":
            em.wait("dve", "s_evp", st["pass_done"][("f1", tile)])
        else:
            em.wait("dve", "s_oadd", st["qa_oadd"][tile])
        for i in range(4):
            em.wait("dve", "s_lnv", base + i)
            em.op("dve", lambda v, i=i, xc=xc: v.bn_stats(
                stats[:, i * 6:(i + 1) * 6], xc[:, i * 512:(i + 1) * 512]),
                inc="s_lnv")
        em.wait("dve", "s_lnv", base + 4)
        em.wait("dve", "s_lnq", g)
        em.op("dve", lambda v: v.bn_aggr(mv[:], stats[:]), inc="s_lnv")
        em.wait("act", "s_lnv", base + 5)
        em.wait("act", "s_norm", 4 * g)
        em.op("act", lambda s: s.activation(rstd[:], mv[:, 1:2], AF.Ln,
                                            bias=epst[:], scale=1.0),
              inc="s_lnq")
        em.wait("act", "s_lnq", em.c("s_lnq"))
        em.op("act", lambda s: s.activation(rstd[:], rstd[:], AF.Exp,
                                            bias=0.0, scale=-0.5),
              inc="s_rstd")
        for grp in range(4):
            gidx = em.c("s_trp")
            par = gidx % 2
            tmv = scr[par][:]
            em.wait("dve", "s_rstd", g + 1)
            em.wait("dve", "s_trp", st["scr_last"][par])
            em.op("dve", lambda v, tmv=tmv, xc=xc, grp=grp:
                  v.tensor_scalar(tmv, xc[:, grp * 512:(grp + 1) * 512],
                                  mv[:, 0:1], rstd[:],
                                  ALU.subtract, ALU.mult), inc="s_norm")
            em.wait("pe", "s_norm", em.c("s_norm"))
            em.wait("pe", "s_evl", gidx - 1)
            if which == "ln2":
                em.wait("pe", "s_qkv", st["xlnattn_reader"])
            else:
                em.wait("pe", "s_mm1", st["xlnf_reader"][which])
            for b in range(4):
                em.op("pe", lambda p, b=b, par=par, tmv=tmv: p.transpose(
                    ps_tr[:, par * 1024 + b * 128:par * 1024 + (b + 1) * 128],
                    tmv[:, b * 128:(b + 1) * 128], idt[:]),
                    inc="s_trp" if b == 3 else None)
            st["scr_last"][par] = em.c("s_trp")
            em.wait("act", "s_trp", gidx + 1)
            dst = dest[:].rearrange("p (hc t) -> p hc t", hc=HC)[
                :, grp * 4:(grp + 1) * 4, tc * 128:(tc + 1) * 128]
            src = ps_tr[:, par * 1024:par * 1024 + 512]\
                .rearrange("p (b t) -> p b t", b=4)
            em.op("act", lambda s, dst=dst, src=src: s.copy(dst, src),
                  inc="s_evl")

    def ln_done(tile, which):
        st["ln_done"][(which, tile)] = em.c("s_evl")

    def ffn_mm1(tile, lnkey):
        base_mm1 = em.c("s_mm1")
        base_gel = em.c("s_gel")
        xln_wait = st["ln_done"][lnkey]
        xln_src = xln_f[lnkey[0]]
        d_guard = st["d_reader"]
        base_w0 = em.c("s_w0")
        wsl_prev = dict(st["wsl_free"])

        def w0_dma(k):
            if k >= 3:
                em.wait("sp", "s_mm1", base_mm1 + 2 * k - 4)
            elif wsl_prev[k % 3] is not None:
                em.wait("sp", *wsl_prev[k % 3])
            em.op("sp", lambda s, k=k: s.dma_start(
                out=wsl[k % 3][:].rearrange("p (a f) -> p a f", a=2),
                in_=w0s[2 * k:2 * k + 2].rearrange("a p f -> p a f")),
                inc="s_w0", inc_n=16)
            st["wsl_free"][k % 3] = ("s_mm1", base_mm1 + 2 * k + 2)
        w0_dma(0)
        w0_dma(1)
        w0_dma(2)
        for j in range(NC):
            if j % 2 == 0 and j // 2 + 3 < NC // 2:
                w0_dma(j // 2 + 3)
            ps = ps_m[j % 2]
            em.wait("pe", "s_w0", base_w0 + 16 * (j // 2 + 1))
            if j == 0:
                em.wait("pe", "s_evl", xln_wait)
            bf = st["bank_free"][j % 2]
            if bf is not None:
                em.wait("pe", bf[0], bf[1])
            for hc in range(HC):
                em.op("pe", lambda pe, j=j, hc=hc, ps=ps: pe.matmul(
                    ps[:, 0:TILE],
                    wsl[(j // 2) % 3][:, (j % 2) * HC * 128 + hc * 128:
                                      (j % 2) * HC * 128 + (hc + 1) * 128],
                    xln_src[:, hc * TILE:(hc + 1) * TILE],
                    start=(hc == 0), stop=(hc == HC - 1)),
                    inc="s_mm1" if hc == HC - 1 else None)
            em.wait("act", "s_mm1", base_mm1 + j + 1)
            if j == 0:
                em.wait("act", "s_mmh", d_guard)
            em.op("act", lambda s, j=j, ps=ps: s.activation(
                dbuf[:, j * TILE:(j + 1) * TILE], ps[:, 0:TILE], AF.Gelu,
                bias=b0sb[:, j:j + 1], scale=1.0), inc="s_gel")
            st["bank_free"][j % 2] = ("s_gel", base_gel + j + 1)
            yield
        st["xlnf_reader"][lnkey[0]] = base_mm1 + NC
        st["gel_end"] = base_gel + NC

    def ffn_mm2(tile, kind):
        base_mmh = em.c("s_mmh")
        base_eev = em.c("s_eev")
        gel_end = st["gel_end"]
        base_w1 = em.c("s_w1")
        wsl_prev = dict(st["wsl_free"])
        xs = x_sl(tile)

        def w1_dma(k):
            if k >= 3:
                em.wait("sp", "s_mmh", base_mmh + k - 2)
            elif wsl_prev[k % 3] is not None:
                em.wait("sp", *wsl_prev[k % 3])
            em.op("sp", lambda s, k=k: s.dma_start(
                out=wsl[k % 3][:],
                in_=w1s[k // 2][:, (k % 2) * 4096:(k % 2) * 4096 + 4096]),
                inc="s_w1", inc_n=16)
            st["wsl_free"][k % 3] = ("s_mmh", base_mmh + k + 1)
        w1_dma(0)
        w1_dma(1)
        w1_dma(2)
        for hc in range(HC):
            ps = ps_m[hc % 2]
            if hc == 0:
                em.wait("pe", "s_gel", gel_end)
            bf = st["bank_free"][hc % 2]
            if bf is not None:
                em.wait("pe", bf[0], bf[1])
            for half in range(2):
                k = 2 * hc + half
                if k + 3 < 2 * HC:
                    w1_dma(k + 3)
                em.wait("pe", "s_w1", base_w1 + 16 * (k + 1))
                for jj in range(32):
                    j = half * 32 + jj
                    em.op("pe", lambda pe, k=k, jj=jj, j=j, ps=ps: pe.matmul(
                        ps[:, 0:TILE], wsl[k % 3][:, jj * 128:(jj + 1) * 128],
                        dbuf[:, j * TILE:(j + 1) * TILE],
                        start=(j == 0), stop=(j == NC - 1)),
                        inc="s_mmh" if jj == 31 else None)
            spar = hc % 2
            em.wait("act", "s_mmh", base_mmh + 2 * (hc + 1))
            em.wait("act", "s_trc", st["scr2_last"][spar])
            em.op("act", lambda s, ps=ps, spar=spar, hc=hc: s.activation(
                scr2[spar][:, 0:TILE], ps[:, 0:TILE], AF.Identity,
                bias=b1h[:, hc:hc + 1], scale=0.5), inc="s_eev")
            tpar = st["te_par"]
            st["te_par"] ^= 1
            em.wait("pe", "s_eev", base_eev + hc + 1)
            if st["te_free"][tpar] is not None:
                em.wait("pe", "s_evp", st["te_free"][tpar])
            for tc in range(NCH):
                em.op("pe", lambda pe, tc=tc, tpar=tpar, spar=spar:
                      pe.transpose(
                          ps_te[:, tpar * 1024 + tc * 128:
                                tpar * 1024 + (tc + 1) * 128],
                          scr2[spar][:, tc * 128:(tc + 1) * 128], idt[:]),
                      inc="s_trc" if tc == NCH - 1 else None)
            st["scr2_last"][spar] = em.c("s_trc")
            st["bank_free"][hc % 2] = ("s_eev", base_eev + hc + 1)
            em.wait("dve", "s_trc", em.c("s_trc"))
            xv = xs.rearrange("p (tc h) -> p tc h", tc=NCH)[
                :, :, hc * 128:(hc + 1) * 128]
            srcv = ps_te[:, tpar * 1024:tpar * 1024 + NCH * 128]\
                .rearrange("p (b t) -> p b t", b=NCH)
            em.op("dve", lambda v, xv=xv, srcv=srcv: v.tensor_tensor(
                xv, xv, srcv, ALU.add), inc="s_evp")
            st["te_free"][tpar] = em.c("s_evp")
            yield
        st["d_reader"] = base_mmh + 2 * HC
        st["pass_done"][(kind, tile)] = em.c("s_evp")

    def qa_tile(i, bg_step, pend_in):
        xs = x_sl(i)
        pend = pend_in
        for tc in range(NCH):
            A = st["qa_idx"]
            st["qa_idx"] += 1
            ctx = ctx2[A % 2]
            ln2_wait = st["ln_done"][("ln2", i)]
            for grp in range(16):
                qg = 16 * A + grp
                qo = st["qo_idx"] % 2
                st["qo_idx"] += 1
                psq = ps_q[qo]
                if grp == 0:
                    em.wait("pe", "s_evl", ln2_wait)
                if st["qo_free"][qo] is not None:
                    em.wait("pe", *st["qo_free"][qo])
                for hh in range(2):
                    c = grp * 2 + hh
                    cb = (c % 2) * 64
                    em.op("pe", lambda pe, c=c, hh=hh, psq=psq: pe.matmul(
                        psq[:, hh * 256:hh * 256 + 192],
                        ones[0:1, 0:128], bq16[0:1, c * 192:(c + 1) * 192],
                        start=True, stop=False))
                    em.op("pe", lambda pe, c=c, hh=hh, cb=cb, psq=psq, tc=tc:
                          pe.matmul(
                              psq[:, hh * 256:hh * 256 + 192],
                              xln_attn[cb:cb + 64,
                                       (c // 2) * TILE + tc * 128:
                                       (c // 2) * TILE + (tc + 1) * 128],
                              wq16[cb:cb + 64, c * 192:(c + 1) * 192],
                              start=False, stop=True),
                          inc="s_qkv" if hh == 1 else None)
                em.wait("act", "s_qkv", qg + 1)
                em.wait("act", "s_ctx", NDC * A)
                dst = qkvt[:].rearrange("p (c i) -> p c i", c=C)[
                    :, grp * 2:grp * 2 + 2, :]
                src = psq[:].rearrange("p (h i) -> p h i", h=2)[:, :, 0:192]
                em.op("act", lambda s, dst=dst, src=src: s.copy(dst, src),
                      inc="s_qev")
                st["qo_free"][qo] = ("s_qev", em.c("s_qev"))
                bg_step(2)
            qr = qkvt[:].rearrange("p (c i) -> p c i", c=C)

            def emit_smul(dc, A=A):
                e_idx = NDC * A + dc
                par = dc % 2
                q_ap = qr[:, :, dc * DD:(dc + 1) * DD]\
                    .unsqueeze(2).broadcast_to([128, C, C, DD])
                k_ap = qr[:, :, 64 + dc * DD:64 + (dc + 1) * DD]\
                    .unsqueeze(1).broadcast_to([128, C, C, DD])
                Sv = sS[par][:].rearrange("p (c e d) -> p c e d", c=C, e=C)
                Ev = sE[par][:].rearrange("p (c e d) -> p c e d", c=C, e=C)
                if dc == 0:
                    em.wait("dve", "s_qev", 16 * (A + 1))
                em.wait("dve", "s_exp", e_idx - 1)
                em.wait("dve", "s_ctx", e_idx - 1)
                em.op("dve", lambda v, Sv=Sv, q_ap=q_ap, k_ap=k_ap:
                      v.tensor_tensor(Sv, q_ap, k_ap, ALU.mult), inc="s_smul")
                em.wait("act", "s_smul", e_idx + 1)
                em.wait("act", "s_g", e_idx - 1)
                em.op("act", lambda s, Ev=Ev, Sv=Sv: s.activation(
                    Ev, Sv, AF.Exp, bias=0.0, scale=0.125), inc="s_exp")

            def emit_zpgc(dc, A=A, ctx=ctx):
                e_idx = NDC * A + dc
                par = dc % 2
                Ev = sE[par][:].rearrange("p (c e d) -> p c e d", c=C, e=C)
                Gv = sS[par][:].rearrange("p (c e d) -> p c e d", c=C, e=C)
                v_ap = qr[:, :, 128 + dc * DD:128 + (dc + 1) * DD]
                em.wait("dve", "s_exp", e_idx + 1)
                em.wait("dve", "s_dvr", e_idx)
                zv = zbuf[:].rearrange("p (c d) -> p c d", c=C)
                em.op("dve", lambda v, zv=zv, Ev=Ev: v.tensor_reduce(
                    zv, Ev.transpose([0, 1, 3, 2]), AX.X, ALU.add),
                    inc="s_dvc")
                if dc + 1 < NDC:
                    emit_smul(dc + 1)
                em.wait("dve", "s_dvc", e_idx + 1)
                em.wait("dve", "s_dvp", e_idx)
                em.op("dve", lambda v: v.reciprocal(zibuf[:], zbuf[:]),
                      inc="s_dvr")
                pv = pbuf[:].rearrange("p (c d) -> p c d", c=C)
                ziv = zibuf[:].rearrange("p (c d) -> p c d", c=C)
                em.wait("dve", "s_dvr", e_idx + 1)
                em.wait("dve", "s_g", e_idx)
                em.op("dve", lambda v, pv=pv, v_ap=v_ap, ziv=ziv:
                      v.tensor_tensor(pv, v_ap, ziv, ALU.mult), inc="s_dvp")
                em.wait("dve", "s_dvp", e_idx + 1)
                em.op("dve", lambda v, Gv=Gv, Ev=Ev, pv=pv: v.tensor_tensor(
                    Gv, Ev, pv.unsqueeze(2).broadcast_to([128, C, C, DD]),
                    ALU.mult), inc="s_g")
                if dc == 0 and st["ctx_free"][A % 2] is not None:
                    em.wait("dve", *st["ctx_free"][A % 2])
                cv = ctx[:].rearrange("p (m d) -> p m d", m=C)[
                    :, :, dc * DD:(dc + 1) * DD]
                em.wait("dve", "s_g", e_idx + 1)

                def _ctxred(v, cv=cv, Gv=Gv):
                    with t["nc"].allow_low_precision(
                            reason="ctx feeds fp16 matmul operand"):
                        return v.tensor_reduce(
                            cv, Gv.transpose([0, 2, 3, 1]), AX.X, ALU.add)
                em.op("dve", _ctxred, inc="s_ctx")

            emit_smul(0)
            for dc in range(NDC):
                emit_zpgc(dc)
                if dc == 2 and pend is not None:
                    pend()
                    pend = None
                bg_step(14)

            def pend_fn(A=A, tc=tc, ctx=ctx):
                for grp in range(4):
                    gidx = em.c("s_trp")
                    par = gidx % 2
                    em.wait("pe", "s_ctx", NDC * (A + 1))
                    em.wait("pe", "s_evl", gidx - 1)
                    for b in range(8):
                        m = grp * 8 + b
                        em.op("pe", lambda pe, m=m, b=b, par=par: pe.transpose(
                            ps_tr[0:64, par * 1024 + b * 128:
                                  par * 1024 + (b + 1) * 128],
                            ctx[:, m * 64:(m + 1) * 64], idt[:]),
                            inc="s_trp" if b == 7 else None)
                    em.op("pe", lambda pe: pe.nop(), inc="s_ctr")
                    em.wait("act", "s_trp", gidx + 1)
                    em.wait("act", "s_op", 4 * A)
                    em.op("act", lambda s, grp=grp, par=par: s.copy(
                        ctxT[0:64, grp * 1024:(grp + 1) * 1024],
                        ps_tr[0:64, par * 1024:par * 1024 + 1024]),
                        inc="s_evl")
                st["ctx_free"][A % 2] = ("s_ctr", em.c("s_ctr"))
                evl_now = em.c("s_evl")
                for grp in range(4):
                    og = 4 * A + grp
                    qo = st["qo_idx"] % 2
                    st["qo_idx"] += 1
                    pso = ps_q[qo]
                    em.wait("pe", "s_evl", evl_now)
                    if st["qo_free"][qo] is not None:
                        em.wait("pe", *st["qo_free"][qo])
                    em.op("pe", lambda pe, grp=grp, pso=pso: pe.matmul(
                        pso[:], ones[0:1, 0:128],
                        bo16[0:1, grp * 512:(grp + 1) * 512],
                        start=True, stop=False))
                    for b in range(8):
                        m = grp * 8 + b
                        em.op("pe", lambda pe, m=m, b=b, pso=pso: pe.matmul(
                            pso[:, b * 64:(b + 1) * 64],
                            ctxT[0:64, m * 128:(m + 1) * 128], wo16[:],
                            start=False, stop=(b == 7)),
                            inc="s_op" if b == 7 else None)
                    em.wait("dve", "s_op", og + 1)
                    xv = xs[:, tc * H + grp * 512: tc * H + (grp + 1) * 512]
                    em.op("dve", lambda v, xv=xv, pso=pso: v.tensor_tensor(
                        xv, xv, pso[:], ALU.add), inc="s_oadd")
                    st["qo_free"][qo] = ("s_oadd", em.c("s_oadd"))
                    bg_step(3)
            pend = pend_fn
        st["xlnattn_reader"] = em.c("s_qkv")
        return pend

    def qa_finish(i, pend):
        if pend is not None:
            pend()
        st["qa_oadd"][i] = em.c("s_oadd")

    x_load(0)
    for tc in range(NCH):
        ln_chunk(0, "ln1", tc)
    ln_done(0, "ln1")
    for _ in ffn_mm1(0, ("ln1", 0)):
        pass
    for _ in ffn_mm2(0, "f1"):
        pass

    for tc in range(NCH):
        ln_chunk(0, "ln2", tc)
    ln_done(0, "ln2")
    for i in range(NT):

        def bg_items(i=i):
            if i >= 1:
                yield from ffn_mm1(i - 1, ("ln3", i - 1))
                yield from ffn_mm2(i - 1, "f2")
                yield lambda: y_store(i - 1)
            if i + 1 < NT:
                yield lambda: x_load(i + 1)
                for tc in range(NCH):
                    yield lambda tc=tc: ln_chunk(i + 1, "ln1", tc)
                yield lambda: ln_done(i + 1, "ln1")
                yield from ffn_mm1(i + 1, ("ln1", i + 1))
                yield from ffn_mm2(i + 1, "f1")
        gen = bg_items()
        n_bg = ((NC + HC + 1) if i >= 1 else 0) + \
               ((1 + NCH + 1 + NC + HC) if i + 1 < NT else 0)
        ticks = NCH * (16 * 2 + NDC * 14)
        emitted = [0]
        tick = [0]

        def bg_step(w):
            tick[0] += w
            target = min(n_bg, (n_bg * tick[0] * 13) // (ticks * 10))
            if target - emitted[0] < 8 and target < n_bg:
                return
            while emitted[0] < target:
                try:
                    item = next(gen)
                except StopIteration:
                    emitted[0] = n_bg
                    return
                if callable(item):
                    item()
                emitted[0] += 1

        pend = qa_tile(i, bg_step, None)
        qa_finish(i, pend)
        for tc in range(NCH):
            ln_chunk(i, "ln3", tc)
        ln_done(i, "ln3")
        for item in gen:
            if callable(item):
                item()
        if i + 1 < NT:
            for tc in range(NCH):
                ln_chunk(i + 1, "ln2", tc)
            ln_done(i + 1, "ln2")

    for _ in ffn_mm1(NT - 1, ("ln3", NT - 1)):
        pass
    for _ in ffn_mm2(NT - 1, "f2"):
        pass
    y_store(NT - 1)

    em.wait("sp", "s_y", 16 * NCH * NT)


def build_nc():
    nc = bass.Bass("TRN2", target_bir_lowering=False)
    t = {}
    t["x_in"] = nc.dram_tensor("x", [TOK, H], F32, kind="ExternalInput").ap()
    t["w0t"] = nc.dram_tensor("w0t", [NC, 128, HC * 128], F32,
                              kind="ExternalInput").ap()
    t["w1t"] = nc.dram_tensor("w1t", [HC, 128, NC * 128], F32,
                              kind="ExternalInput").ap()
    t["wq_in"] = nc.dram_tensor("wq", [64, C * 192], F32,
                                kind="ExternalInput").ap()
    t["bq_in"] = nc.dram_tensor("bq", [1, C * 192], F32,
                                kind="ExternalInput").ap()
    t["b0_in"] = nc.dram_tensor("b0", [128, NC], F32, kind="ExternalInput").ap()
    t["b1_in"] = nc.dram_tensor("b1", [128, HC], F32, kind="ExternalInput").ap()
    t["wo_in"] = nc.dram_tensor("wo", [64, 64], F32, kind="ExternalInput").ap()
    t["bo_in"] = nc.dram_tensor("bo", [1, H], F32, kind="ExternalInput").ap()
    t["id_in"] = nc.dram_tensor("ident", [128, 128], F16,
                                kind="ExternalInput").ap()
    t["y_out"] = nc.dram_tensor("y", [TOK, H], F32, kind="ExternalOutput").ap()
    t["w0s"] = nc.dram_tensor("w0s", [NC, 128, HC * 128], F16,
                              kind="Internal").ap()
    t["w1s"] = nc.dram_tensor("w1s", [HC, 128, NC * 128], F16,
                              kind="Internal").ap()

    sbufs = [
        ("xbuf", [128, 2 * NCH * H], F32),
        ("xln_f1", [128, HC * TILE], F16),
        ("xln_f3", [128, HC * TILE], F16),
        ("xln_attn", [128, HC * TILE], F16),
        ("dbuf", [128, NC * TILE], F16),
        ("wsl0", [128, 4096], F16), ("wsl1", [128, 4096], F16),
        ("wsl2", [128, 4096], F16),
        ("qkvt", [128, C * 192], F16),
        ("sS0", [128, C * C * DD], F16), ("sS1", [128, C * C * DD], F16),
        ("sE0", [128, C * C * DD], F16), ("sE1", [128, C * C * DD], F16),
        ("scr0", [128, 512], F16), ("scr1", [128, 512], F16),
        ("scr2a", [128, 512], F16), ("scr2b", [128, 512], F16),
        ("zbuf", [128, C * DD], F32), ("zibuf", [128, C * DD], F32),
        ("zl", [128, C * DD], F32),
        ("pbuf", [128, C * DD], F16),
        ("ctx0", [128, C * HD], F16), ("ctx1", [128, C * HD], F16),
        ("ctxT", [64, C * 128], F16),
        ("wq16", [128, C * 192], F16), ("bq16", [1, C * 192], F16),
        ("bo16", [1, H], F16), ("wo16", [64, 64], F16),
        ("b0sb", [128, NC], F32), ("b1sb", [128, HC], F32),
        ("b1h", [128, HC], F32),
        ("idt", [128, 128], F16), ("ones", [1, 128], F16),
        ("stats", [128, 24], F32), ("mv", [128, 2], F32),
        ("rstd", [128, 1], F32), ("epst", [128, 1], F32),
    ]
    psums = [
        ("ps_m0", [128, 512], F32), ("ps_m1", [128, 512], F32),
        ("ps_tr", [128, 2048], F16),
        ("ps_te", [128, 2048], F16),
        ("ps_q0", [128, 512], F32), ("ps_q1", [128, 512], F32),
    ]

    with contextlib.ExitStack() as stack:
        for name, shape, dt in sbufs:
            t[name] = stack.enter_context(nc.sbuf_tensor(name, shape, dt))[:]
        for name, shape, dt in psums:
            t[name] = stack.enter_context(nc.psum_tensor(name, shape, dt))[:]
        t["nc"] = nc
        sems = {}
        for s in SEM_NAMES:
            sems[s] = stack.enter_context(nc.semaphore(s))
        with nc.Block() as block:
            block.sync(lambda eng: schedule(Emit(sems, "sp", eng), t))
            block.gpsimd(lambda eng: schedule(Emit(sems, "pool", eng), t))
            block.vector(lambda eng: schedule(Emit(sems, "dve", eng), t))
            block.scalar(lambda eng: schedule(Emit(sems, "act", eng), t))
            block.tensor(lambda eng: schedule(Emit(sems, "pe", eng), t))
    return nc


def _host_layouts(L0_attn_Wqkv, L0_attn_bqkv, L0_attn_Wout, L0_attn_bout,
                  L0_ffn_W0, L0_ffn_bias0, L0_ffn_W1, L0_ffn_bias1):
    W0 = np.asarray(L0_ffn_W0, dtype=np.float32)
    W1 = np.asarray(L0_ffn_W1, dtype=np.float32)
    w0t = np.ascontiguousarray(
        W0.reshape(NC, 128, HC, 128).transpose(0, 3, 2, 1)
        .reshape(NC, 128, HC * 128))
    w1t = np.ascontiguousarray(
        W1.reshape(HC, 128, NC, 128).transpose(0, 3, 2, 1)
        .reshape(HC, 128, NC * 128))
    wq = np.ascontiguousarray(
        np.asarray(L0_attn_Wqkv, dtype=np.float32).transpose(1, 0, 2)
        .reshape(64, C * 192))
    bq = np.ascontiguousarray(
        np.asarray(L0_attn_bqkv, dtype=np.float32).reshape(1, C * 192))
    b0 = np.ascontiguousarray(
        np.asarray(L0_ffn_bias0, dtype=np.float32).reshape(NC, 128).T)
    b1 = np.ascontiguousarray(
        np.asarray(L0_ffn_bias1, dtype=np.float32).reshape(HC, 128).T)
    wo = np.ascontiguousarray(np.asarray(L0_attn_Wout, dtype=np.float32))
    bo = np.ascontiguousarray(
        np.tile(np.asarray(L0_attn_bout, dtype=np.float32), C).reshape(1, H))
    ident = np.eye(128, dtype=np.float16)
    return dict(w0t=w0t, w1t=w1t, wq=wq, bq=bq, b0=b0, b1=b1, wo=wo, bo=bo,
                ident=ident)


def kernel(L0_attn_x, L0_attn_Wqkv, L0_attn_bqkv, L0_attn_Wout, L0_attn_bout,
           L0_ffn_W0, L0_ffn_bias0, L0_ffn_W1, L0_ffn_bias1):
    x = np.ascontiguousarray(np.asarray(L0_attn_x, dtype=np.float32))
    consts = _host_layouts(L0_attn_Wqkv, L0_attn_bqkv, L0_attn_Wout,
                           L0_attn_bout, L0_ffn_W0, L0_ffn_bias0, L0_ffn_W1,
                           L0_ffn_bias1)
    nc = build_nc()
    in_maps = [dict(x=np.ascontiguousarray(x[i]), **consts)
               for i in range(NCORES)]
    res = run_bass_kernel_spmd(nc, in_maps, core_ids=list(range(NCORES)))
    return np.stack([r["y"] for r in res.results], axis=0)
